# version 12
# baseline (speedup 1.0000x reference)
import numpy as np
import ml_dtypes
from collections import defaultdict
from contextlib import ExitStack

import concourse.bass as bass
import concourse.mybir as mybir
from concourse.bass_utils import run_bass_kernel_spmd

F32 = mybir.dt.float32
BF16 = mybir.dt.bfloat16
AF = mybir.ActivationFunctionType
ALU = mybir.AluOpType
AXX = mybir.AxisListType.X

BF = ml_dtypes.bfloat16

S = 2048
D = 2048
H = 16
HD = 128
DKV = 512
DQ = 1536
INTER = 5632
TQ = 1024
NKC = D // 128
NIC = INTER // 128
NEGB = -15.0
SCALE = 1.0 / 16.0
SHUF_SWAP = [i ^ 1 for i in range(32)]

_BCOLS = {}
_nb = 0
def _bc(name, n):
    global _nb
    _BCOLS[name] = _nb
    _nb += n
_bc("bdkv", 4); _bc("bdq", 12); _bc("buk", 16); _bc("buq", 16); _bc("bkr", 16)
_bc("bo", 16); _bc("gs", 44); _bc("us", 44); _bc("ds", 16); _bc("negb", 1)
_bc("eps", 1)
NBIAS = _nb


class Sched:
    ENGS = ("pe", "dv", "ac", "sy", "gp")

    def __init__(self):
        self.ops = []
        self.cnt = defaultdict(int)

    def do(self, eng, fn, inc=None, n=1):
        assert eng in self.ENGS
        self.ops.append((eng, "op", fn, (inc, n) if inc else None))
        if inc:
            self.cnt[inc] += n
            return self.cnt[inc]
        return None

    def wait(self, eng, sem, val=None):
        if val is None:
            val = self.cnt[sem]
        if val is None or val <= 0:
            return
        self.ops.append((eng, "wait", sem, val))

    def now(self, sem):
        return self.cnt[sem]

    def barrier(self):
        snap = {s: self.cnt[s] for s in ("PE", "DV", "AC", "LD", "WT")}
        for e in self.ENGS:
            for s_, v in snap.items():
                self.wait(e, s_, v)


class Arena:

    def __init__(self, handle, total):
        self.ap_full = handle.ap()
        self.total = total
        self.off = 0
        self.marks = []

    def bf(self, n):
        n = (n + 63) & ~63
        assert self.off + n <= self.total, f"arena overflow {self.off}+{n}"
        ap = self.ap_full[:, self.off:self.off + n]
        self.off += n
        return ap

    def f32(self, n):
        n2 = 2 * ((n + 31) & ~31)
        assert self.off + n2 <= self.total, f"arena overflow {self.off}+{n2}"
        ap = self.ap_full[:, self.off:self.off + n2].bitcast(F32)
        self.off += n2
        return ap

    def at_bf(self, off, n):
        assert off + n <= self.total
        return self.ap_full[:, off:off + n]

    def mark(self):
        self.marks.append(self.off)

    def release(self):
        self.off = self.marks.pop()


class PsumPool:

    def __init__(self, sched, tensors):
        self.s = sched
        self.P = tensors
        self.free_at = [None] * 8

    def unit(self, u):
        return self.P[u // 2][:, (u % 2) * 512:(u % 2) * 512 + 512]

    def acquire(self, units):
        for u in units:
            fa = self.free_at[u]
            if fa:
                self.s.wait("pe", fa[0], fa[1])
                self.free_at[u] = None

    def release(self, units, sem, cnt):
        for u in units:
            self.free_at[u] = (sem, cnt)


def seg512(a, b):
    out = []
    while a < b:
        e = min((a // 512 + 1) * 512, b)
        out.append((a, e))
        a = e
    return out


class _Stop(Exception):
    pass


def build_program(taps=(), stop_after=None):
    nc = bass.Bass("TRN2", target_bir_lowering=False)
    taps = set(taps)

    x_d = nc.dram_tensor("x", [S, D], F32, kind="ExternalInput").ap()
    wdkv_d = nc.dram_tensor("wdkv", [D, DKV], F32, kind="ExternalInput").ap()
    wdq_d = nc.dram_tensor("wdq", [D, DQ], F32, kind="ExternalInput").ap()
    wuk_d = nc.dram_tensor("wuk", [DKV, D], F32, kind="ExternalInput").ap()
    wuv_d = nc.dram_tensor("wuv", [DKV, D], F32, kind="ExternalInput").ap()
    wuq_d = nc.dram_tensor("wuq", [DQ, D], F32, kind="ExternalInput").ap()
    wkr_d = nc.dram_tensor("wkr", [D, D], F32, kind="ExternalInput").ap()
    wo_d = nc.dram_tensor("wo", [D, D], F32, kind="ExternalInput").ap()
    wg_d = nc.dram_tensor("wg", [D, INTER], F32, kind="ExternalInput").ap()
    wu_d = nc.dram_tensor("wu", [D, INTER], F32, kind="ExternalInput").ap()
    wd_d = nc.dram_tensor("wd", [INTER, D], F32, kind="ExternalInput").ap()
    biasT_d = nc.dram_tensor("biasT", [128, NBIAS], F32, kind="ExternalInput").ap()
    ident_d = nc.dram_tensor("ident", [128, 128], BF16, kind="ExternalInput").ap()
    tri_d = nc.dram_tensor("tri", [128, 128], BF16, kind="ExternalInput").ap()
    pmask_d = nc.dram_tensor("pmask", [128, 128], BF16, kind="ExternalInput").ap()
    c2k_d = nc.dram_tensor("c2k", [128, S], BF16, kind="ExternalInput").ap()
    s2k_d = nc.dram_tensor("s2k", [128, S], BF16, kind="ExternalInput").ap()
    buv_d = nc.dram_tensor("buvrow", [1, D], BF16, kind="ExternalInput").ap()
    ctx_hbm = nc.dram_tensor(
        "ctxscr", [H * HD, TQ], BF16,
        kind=("ExternalOutput" if "ctx" in taps else "Internal")).ap()
    out_d = nc.dram_tensor("out", [TQ, D], F32, kind="ExternalOutput").ap()
    lscr_d = nc.dram_tensor("lscr", [1, TQ], F32, kind="Internal").ap()

    tap_d = {}
    def decl_tap(name, shape, dt):
        if name in taps:
            tap_d[name] = nc.dram_tensor("tap_" + name, shape, dt,
                                         kind="ExternalOutput").ap()
    decl_tap("xnT", [128, NKC * S], BF16)
    decl_tap("ckvT", [128, 4 * S], BF16)
    decl_tap("cqT", [128, 12 * TQ], BF16)
    decl_tap("kCT", [128, S], BF16)
    decl_tap("kRT", [128, S], BF16)
    decl_tap("vG", [128, 16 * 512], BF16)
    decl_tap("qCT", [128, TQ], BF16)
    decl_tap("qRT", [128, TQ], BF16)
    decl_tap("sumT", [128, NKC * TQ], BF16)
    decl_tap("sumT2", [128, NKC * TQ], BF16)

    sch = Sched()

    with ExitStack() as es:
        e = es.enter_context
        ARENA_N = 106368
        arena_h = e(nc.sbuf_tensor("arena", [128, ARENA_N], BF16))
        A = Arena(arena_h, ARENA_N)
        P = [e(nc.psum_tensor(f"psum{i}", [128, 1024], F32)).ap()
             for i in range(4)]
        pool = PsumPool(sch, P)
        sems = {s_: e(nc.semaphore("sem" + s_))
                for s_ in ("LD", "WT", "PE", "DV", "AC")}

        biases = A.f32(NBIAS)
        ident = A.bf(128)
        ones_bf = A.bf(64)[:, 0:1]
        ones1_bf = A.bf(128)[0:1, :]
        ones1_f = A.f32(128)[0:1, :]
        xnT_off = A.off
        xnT = A.bf(NKC * S)

        def xnt(kc):
            return xnT[:, kc * S:(kc + 1) * S]

        def bcol(name, i=0):
            c = _BCOLS[name] + i
            return biases[:, c:c + 1]

        def MM(out, lh, rh, start, stop, inc=None):
            return sch.do("pe", lambda t, o=out, l=lh, r=rh, s1=start, s2=stop:
                          t.matmul(o, l, r, start=s1, stop=s2), inc)

        def ACT(out, in_, func, bias=0.0, scale=1.0, accum=None, inc="AC"):
            return sch.do("ac", lambda a, o=out, i=in_, f=func, b=bias, s=scale,
                          ac_=accum: a.activation(out=o, in_=i, func=f, bias=b,
                                                  scale=s, accum_out=ac_), inc)

        def DV(meth, *args, inc="DV", **kw):
            def fn(v, m=meth, a=args, k=dict(kw)):
                return getattr(v, m)(*a, **k)
            return sch.do("dv", fn, inc)

        def DMA(q, out, in_, sem):
            return sch.do(q, lambda g, o=out, i=in_: g.dma_start(out=o, in_=i),
                          sem, 16)

        def tap_dump(name, src_ap):
            if name in tap_d:
                sch.barrier()
                DMA("sy", tap_d[name], src_ap, "LD")

        def wslice(dram, col0, ncols, nch, p=128):
            return dram[:, col0:col0 + ncols].rearrange(
                "(kc p) c -> p kc c", p=p)

        def r3(sbuf_ap, ncols):
            return sbuf_ap.rearrange("p (kc c) -> p kc c", c=ncols)

        _trunc = {}

        def ckstop(ph):
            if stop_after == ph and not _trunc:
                _trunc["n"] = len(sch.ops)
                _trunc["ld"] = sch.cnt["LD"]
                _trunc["wt"] = sch.cnt["WT"]

        DMA("sy", biases[:, 0:NBIAS], biasT_d, "LD")
        DMA("sy", ident, ident_d, "LD")
        setup_ld = sch.now("LD")
        DV("memset", ones_bf, 1.0)
        DV("memset", ones1_bf, 1.0)
        DV("memset", ones1_f, 1.0)

        A.mark()
        tri = A.bf(128)
        pmask = A.bf(128)
        c2k = A.bf(S)
        s2k = A.bf(S)
        buv_row = A.bf(D)[0:1, :]
        ckvT = A.bf(4 * S)
        cqT = A.bf(12 * TQ)
        DMA("sy", tri, tri_d, "LD")
        DMA("sy", pmask, pmask_d, "LD")
        DMA("sy", c2k, c2k_d, "LD")
        DMA("sy", s2k, s2k_d, "LD")
        DMA("sy", buv_row, buv_d, "LD")
        tables_ld = sch.now("LD")

        A.mark()
        AR = 4
        x_t = [A.f32(D) for _ in range(AR)]
        xn_tm = [A.bf(D) for _ in range(AR)]
        sfull = A.f32(32 * AR)
        st6 = [sfull[:, 32 * i:32 * i + 24] for i in range(AR)]
        st2c = [sfull[:, 32 * i + 24:32 * i + 26] for i in range(AR)]
        wdkv_all = A.bf(4 * NKC * 128)
        wdq_r = [A.bf(NKC * 128) for _ in range(4)]

        a_dv_done = {}
        a_ac_done = {}
        a_pe_done = {}
        sch.wait("dv", "LD", setup_ld)
        sch.wait("pe", "LD", setup_ld)
        sch.wait("ac", "LD", setup_ld)
        for oc in range(4):
            wt = DMA("gp", r3(wdkv_all[:, oc * NKC * 128:(oc + 1) * NKC * 128],
                              128),
                     wslice(wdkv_d, oc * 128, 128, NKC), "WT")
        wdkv_wt = sch.now("WT")
        wdq_wt = {}
        wdq_pe = {}

        def emit_tile(t):
            r = t % AR
            if t >= AR:
                sch.wait("sy", "DV", a_dv_done[t - AR])
            ld = DMA("sy", x_t[r], x_d[t * 128:(t + 1) * 128, :], "LD")
            sch.wait("dv", "LD", ld)
            for j in range(4):
                DV("bn_stats", st6[r][:, 6 * j:6 * j + 6],
                   x_t[r][:, 512 * j:512 * j + 512])
            dv_agg = DV("bn_aggr", st2c[r], st6[r])
            sch.wait("ac", "DV", dv_agg)
            ac_ln = ACT(st2c[r][:, 1:2], st2c[r][:, 1:2], AF.Ln,
                        bias=bcol("eps"))
            sch.wait("ac", "AC", ac_ln)
            ac_rstd = ACT(st2c[r][:, 1:2], st2c[r][:, 1:2], AF.Exp,
                          scale=-0.5)
            sch.wait("dv", "AC", ac_rstd)
            if t >= AR:
                sch.wait("dv", "PE", a_pe_done[t - AR])
            dv_app = DV("tensor_scalar", xn_tm[r], x_t[r], st2c[r][:, 0:1],
                        st2c[r][:, 1:2], ALU.subtract, ALU.mult)
            a_dv_done[t] = dv_app
            sch.wait("pe", "DV", dv_app)
            for half in range(2):
                units = (0, 1) if half == 0 else (2, 3)
                pool.acquire(units)
                for j in range(8):
                    kc = 8 * half + j
                    u = units[j // 4]
                    dst = pool.unit(u)[:, (j % 4) * 128:(j % 4) * 128 + 128]
                    pe_last = MM(dst, xn_tm[r][:, kc * 128:(kc + 1) * 128],
                                 ident, start=(j % 4 == 0), stop=(j % 4 == 3),
                                 inc="PE" if j == 7 else None)
                if half == 1:
                    a_pe_done[t] = pe_last
                sch.wait("ac", "PE", pe_last)
                pv = P[0] if half == 0 else P[1]
                src_ = pv.rearrange("p (kc c) -> p kc c", c=128)
                dst_ = xnT.rearrange("p (kc tt) -> p kc tt", tt=S)[
                    :, 8 * half:8 * half + 8, t * 128:(t + 1) * 128]
                ac_ev = ACT(dst_, src_, AF.Identity)
                pool.release(units, "AC", ac_ev)
            a_ac_done[t] = ac_ev

        def emit_wdq_dma(n, oc):
            prev = wdq_pe.get((n, oc - 4)) if oc >= 4 else (
                wdq_pe.get((0, oc + 8)) if n == 1 else None)
            if prev:
                sch.wait("gp", "PE", prev)
            wdq_wt[(n, oc)] = DMA("gp", r3(wdq_r[oc % 4], 128),
                                  wslice(wdq_d, oc * 128, 128, NKC), "WT")

        def emit_bsec(n):
            sch.wait("pe", "AC", a_ac_done[4 * n + 3])
            if n == 0:
                sch.wait("pe", "WT", wdkv_wt)
            chunks = [("kv", oc) for oc in range(4)]
            if n < 2:
                chunks += [("q", oc) for oc in range(12)]
                for oc in range(4):
                    emit_wdq_dma(n, oc)
            for k, (kind, oc) in enumerate(chunks):
                u = 4 + k % 4
                pool.acquire((u,))
                if kind == "q":
                    sch.wait("pe", "WT", wdq_wt[(n, oc)])
                for kc in range(NKC):
                    lh = (wdkv_all[:, (oc * NKC + kc) * 128:
                                   (oc * NKC + kc) * 128 + 128]
                          if kind == "kv" else
                          wdq_r[oc % 4][:, kc * 128:(kc + 1) * 128])
                    pe_l = MM(pool.unit(u), lh,
                              xnt(kc)[:, n * 512:(n + 1) * 512],
                              start=(kc == 0), stop=(kc == NKC - 1),
                              inc="PE" if kc == NKC - 1 else None)
                if kind == "q":
                    wdq_pe[(n, oc)] = pe_l
                    if oc + 4 < 12:
                        emit_wdq_dma(n, oc + 4)
                sch.wait("ac", "PE", pe_l)
                if kind == "kv":
                    dst = ckvT[:, oc * S + n * 512: oc * S + (n + 1) * 512]
                    b = bcol("bdkv", oc)
                else:
                    dst = cqT[:, oc * TQ + n * 512: oc * TQ + (n + 1) * 512]
                    b = bcol("bdq", oc)
                ac_ev = ACT(dst, pool.unit(u), AF.Identity, bias=b)
                pool.release((u,), "AC", ac_ev)

        for t in range(16):
            emit_tile(t)
            if t % 4 == 3:
                emit_bsec(t // 4)
        A.release()
        tap_dump("xnT", xnT)
        tap_dump("ckvT", ckvT)
        tap_dump("cqT", cqT)
        ckstop("B")

        sch.barrier()
        A.mark()
        kRT = [A.bf(S), A.bf(S)]
        kCT = [A.bf(S), A.bf(S)]
        vG = A.bf(16 * 512)
        qCT = [A.bf(TQ), A.bf(TQ)]
        qRT = [A.bf(TQ), A.bf(TQ)]
        swp = A.bf(S)
        Et = [A.bf(TQ), A.bf(TQ)]
        linv_rep = A.f32(TQ)
        ctx_h = A.bf(TQ)
        ctx_n = A.bf(TQ)
        wkr_r = [A.bf(NKC * 128), A.bf(NKC * 128)]
        wuk_r = [A.bf(4 * 128), A.bf(4 * 128)]
        wuv_r = A.bf(4 * 512)
        wuq_r = [A.bf(12 * 128), A.bf(12 * 128)]

        sch.wait("dv", "LD", tables_ld)
        sch.wait("pe", "LD", tables_ld)

        v_pe_done = {}
        att_end_pe = {}
        kr_pe_d = {}
        kc_pe_d = {}
        q_pe_d = {}
        dv_qrope_d = {}
        ac_kc_d = {}
        dv_ctx_d = {}
        lout_ld = {}
        lbcast_ld = {}
        ctx_store = {}

        def emit_vgroup(g):
            if g > 0:
                sch.wait("gp", "PE", v_pe_done[g - 1])
            wt = DMA("gp", r3(wuv_r, 512), wslice(wuv_d, g * 512, 512, 4),
                     "WT")
            sch.wait("pe", "WT", wt)
            if g > 0:
                sch.wait("ac", "PE", att_end_pe[4 * g - 1])
            uperm = (0, 1, 2, 3, 6, 7, 4, 5)
            for kb in range(16):
                u = uperm[kb % 8]
                pool.acquire((u,))
                for kc in range(4):
                    MM(pool.unit(u),
                       ckvT[:, kc * S + kb * 128: kc * S + kb * 128 + 128],
                       wuv_r[:, kc * 512:(kc + 1) * 512],
                       start=(kc == 0), stop=False)
                pe_l = MM(pool.unit(u), ones1_bf,
                          buv_row[:, g * 512:(g + 1) * 512],
                          start=False, stop=True, inc="PE")
                sch.wait("ac", "PE", pe_l)
                ac_ev = ACT(vG[:, kb * 512:(kb + 1) * 512], pool.unit(u),
                            AF.Identity)
                pool.release((u,), "AC", ac_ev)
            v_pe_done[g] = sch.now("PE")
            if g == 0:
                tap_dump("vG", vG)

        def emit_prod(hh):
            r = hh % 2
            if hh >= 2:
                sch.wait("ac", "PE", att_end_pe[hh - 2])
                sch.wait("dv", "PE", att_end_pe[hh - 2])
                sch.wait("gp", "PE", kr_pe_d[hh - 2])
            wt = DMA("gp", r3(wkr_r[r], 128),
                     wslice(wkr_d, hh * 128, 128, NKC), "WT")
            units = (0, 1, 2, 3)
            pool.acquire(units)
            sch.wait("pe", "WT", wt)
            for kc in range(NKC):
                lh = wkr_r[r][:, kc * 128:(kc + 1) * 128]
                for n in range(4):
                    pe_l = MM(pool.unit(units[n]),
                              lh, xnt(kc)[:, n * 512:(n + 1) * 512],
                              start=(kc == 0), stop=(kc == NKC - 1),
                              inc="PE" if (kc == NKC - 1 and n == 3) else None)
            kr_pe_d[hh] = pe_l
            sch.wait("ac", "PE", pe_l)
            for half in range(2):
                ac_ev = ACT(kRT[r][:, half * 1024:half * 1024 + 1024],
                            P[units[2 * half] // 2], AF.Identity,
                            bias=bcol("bkr", hh))
            pool.release(units, "AC", ac_ev)
            sch.wait("dv", "AC", ac_ev)
            DV("stream_shuffle", swp, kRT[r], SHUF_SWAP)
            DV("tensor_tensor", kRT[r], kRT[r], c2k, ALU.mult)
            DV("tensor_tensor", swp, swp, s2k, ALU.mult)
            DV("tensor_tensor", kRT[r], kRT[r], swp, ALU.add)
            if hh >= 2:
                sch.wait("gp", "PE", kc_pe_d[hh - 2])
            wt = DMA("gp", r3(wuk_r[r], 128),
                     wslice(wuk_d, hh * 128, 128, 4), "WT")
            units = (4, 5, 6, 7)
            pool.acquire(units)
            sch.wait("pe", "WT", wt)
            for kc in range(4):
                lh = wuk_r[r][:, kc * 128:(kc + 1) * 128]
                for n in range(4):
                    pe_l = MM(pool.unit(units[n]), lh,
                              ckvT[:, kc * S + n * 512: kc * S + n * 512 + 512],
                              start=(kc == 0), stop=(kc == 3),
                              inc="PE" if (kc == 3 and n == 3) else None)
            kc_pe_d[hh] = pe_l
            sch.wait("ac", "PE", pe_l)
            for half in range(2):
                ac_kc = ACT(kCT[r][:, half * 1024:half * 1024 + 1024],
                            P[units[2 * half] // 2], AF.Identity,
                            bias=bcol("buk", hh))
            pool.release(units, "AC", ac_kc)
            ac_kc_d[hh] = ac_kc
            if hh >= 2:
                sch.wait("gp", "PE", q_pe_d[hh - 2])
            wt = DMA("gp", r3(wuq_r[r], 128),
                     wslice(wuq_d, hh * 128, 128, 12), "WT")
            units = (0, 1)
            pool.acquire(units)
            sch.wait("pe", "WT", wt)
            for kc in range(12):
                lh = wuq_r[r][:, kc * 128:(kc + 1) * 128]
                for n in range(2):
                    pe_l = MM(pool.unit(units[n]), lh,
                              cqT[:, kc * TQ + n * 512: kc * TQ + n * 512 + 512],
                              start=(kc == 0), stop=(kc == 11),
                              inc="PE" if (kc == 11 and n == 1) else None)
            q_pe_d[hh] = pe_l
            sch.wait("ac", "PE", pe_l)
            ac_q = ACT(qCT[r], P[0], AF.Identity, bias=bcol("buq", hh))
            pool.release(units, "AC", ac_q)
            sch.wait("dv", "AC", ac_q)
            DV("stream_shuffle", swp[:, 0:TQ], qCT[r], SHUF_SWAP)
            DV("tensor_tensor", qRT[r], qCT[r], c2k[:, 0:TQ], ALU.mult)
            DV("tensor_tensor", swp[:, 0:TQ], swp[:, 0:TQ], s2k[:, 0:TQ],
               ALU.mult)
            dv_qrope_d[hh] = DV("tensor_tensor", qRT[r], qRT[r],
                                swp[:, 0:TQ], ALU.add)
            if hh == 0:
                tap_dump("kRT", kRT[r])
                tap_dump("kCT", kCT[r])
                tap_dump("qCT", qCT[r])
                tap_dump("qRT", qRT[r])

        def emit_att(hh):
            oc = hh % 4
            r = hh % 2
            pool.acquire((0, 1, 2, 3))
            sch.wait("pe", "DV", dv_qrope_d[hh])
            sch.wait("pe", "AC", ac_kc_d[hh])
            ctxl_acquired = [False]
            exp_done = {}
            ctxl_done = {}
            last_kb_bank = {0: 11, 1: 15}

            def emit_ctxl(kb):
                j0 = kb if kb < 8 else kb - 8
                q0 = j0 * 128
                par = kb % 2
                segs = seg512(q0, TQ)
                if not ctxl_acquired[0]:
                    pool.acquire((4, 5, 6, 7))
                    ctxl_acquired[0] = True
                sch.wait("pe", "AC", exp_done[kb])
                for a_, b_ in segs:
                    bank = a_ // 512
                    MM(P[2][:, a_:b_],
                       vG[:, kb * 512 + oc * 128: kb * 512 + oc * 128 + 128],
                       Et[par][:, a_:b_], start=(kb == 0),
                       stop=(kb == last_kb_bank[bank]))
                for i_, (a_, b_) in enumerate(segs):
                    bank = a_ // 512
                    pe_cl = MM(P[3][0:1, a_:b_], ones_bf, Et[par][:, a_:b_],
                               start=(kb == 0),
                               stop=(kb == last_kb_bank[bank]),
                               inc="PE" if i_ == len(segs) - 1 else None)
                ctxl_done[kb] = pe_cl

            for kb in range(16):
                j0 = kb if kb < 8 else kb - 8
                q0 = j0 * 128
                par = kb % 2
                Spar = P[par]
                if kb >= 2:
                    sch.wait("pe", "AC", exp_done[kb - 2])
                segs = seg512(q0, TQ)
                q0bank = q0 // 512
                mk = tri if kb < 8 else pmask
                MM(Spar[:, q0:q0 + 128], ident, mk, start=True, stop=False)
                for a_, b_ in segs:
                    MM(Spar[:, a_:b_],
                       kCT[r][:, kb * 128:kb * 128 + 128],
                       qCT[r][:, a_:b_],
                       start=(a_ // 512 != q0bank), stop=False)
                for i_, (a_, b_) in enumerate(segs):
                    pe_sc = MM(Spar[:, a_:b_],
                               kRT[r][:, kb * 128:kb * 128 + 128],
                               qRT[r][:, a_:b_], start=False, stop=True,
                               inc="PE" if i_ == len(segs) - 1 else None)
                sch.wait("ac", "PE", pe_sc)
                if kb >= 2:
                    sch.wait("ac", "PE", ctxl_done[kb - 2])
                ac_exp = ACT(Et[par][:, q0:TQ], Spar[:, q0:TQ], AF.Exp,
                             bias=bcol("negb"), scale=SCALE)
                exp_done[kb] = ac_exp
                if kb >= 1:
                    emit_ctxl(kb - 1)
            emit_ctxl(15)
            sch.wait("ac", "PE", ctxl_done[15])
            if hh >= 1:
                sch.wait("ac", "DV", dv_ctx_d[hh - 1])
            ac_cu = ACT(ctx_h, P[2], AF.Identity)
            if hh >= 1:
                sch.wait("ac", "LD", lbcast_ld[hh - 1])
            ac_ln = ACT(linv_rep[0:1, :], P[3][0:1, :], AF.Ln)
            sch.wait("ac", "AC", ac_ln)
            ac_ex = ACT(linv_rep[0:1, :], linv_rep[0:1, :], AF.Exp,
                        scale=-1.0)
            if hh >= 1:
                sch.wait("sy", "LD", lbcast_ld[hh - 1])
            sch.wait("sy", "AC", ac_ex)
            lout_ld[hh] = DMA("sy", lscr_d, linv_rep[0:1, :], "LD")
            if hh >= 1:
                sch.wait("sy", "DV", dv_ctx_d[hh - 1])
            sch.wait("sy", "LD", lout_ld[hh])
            lbcast_ld[hh] = DMA("sy", linv_rep,
                                lscr_d.broadcast_to([128, TQ]), "LD")
            sch.wait("dv", "LD", lbcast_ld[hh])
            if hh >= 1:
                sch.wait("dv", "LD", ctx_store[hh - 1])
            sch.wait("dv", "AC", ac_cu)
            dv_ctx = DV("tensor_tensor", ctx_n, ctx_h, linv_rep, ALU.mult)
            dv_ctx_d[hh] = dv_ctx
            sch.wait("sy", "DV", dv_ctx)
            ctx_store[hh] = DMA("sy", ctx_hbm[hh * 128:(hh + 1) * 128, :],
                                ctx_n, "LD")
            att_end_pe[hh] = ctxl_done[15]
            pool.release((0, 1), "AC", exp_done[14])
            pool.release((2, 3), "AC", exp_done[15])
            pool.release((4, 5), "AC", ac_cu)
            pool.release((6, 7), "AC", ac_ln)

        for g in range(4):
            emit_vgroup(g)
            for oc in range(4):
                hh = 4 * g + oc
                emit_prod(hh)
                if oc >= 1:
                    emit_att(hh - 1)
            emit_att(4 * g + 3)
        A.release()
        A.release()
        ckstop("C")

        sch.wait("gp", "PE")
        sch.wait("gp", "DV")
        sch.wait("sy", "DV")
        sch.wait("sy", "AC")
        sch.wait("sy", "PE")
        sumT = A.bf(NKC * TQ)
        A.mark()
        ctxT = A.bf(NKC * TQ)
        wo_r = [A.bf(NKC * 128), A.bf(NKC * 128)]
        for kc in range(NKC):
            DMA("sy", ctxT[:, kc * TQ:(kc + 1) * TQ],
                ctx_hbm[kc * 128:(kc + 1) * 128, :], "LD")
        ctx_ld = sch.now("LD")
        sch.wait("pe", "LD", ctx_ld)
        d_pe_done = {}
        for oc in range(NKC):
            r = oc % 2
            if oc >= 2:
                sch.wait("gp", "PE", d_pe_done[oc - 2])
            wt = DMA("gp", r3(wo_r[r], 128), wslice(wo_d, oc * 128, 128, NKC), "WT")
            units = (0, 1) if r == 0 else (2, 3)
            pool.acquire(units)
            sch.wait("pe", "WT", wt)
            for kc in range(NKC):
                lh = wo_r[r][:, kc * 128:(kc + 1) * 128]
                for n in range(2):
                    pe_l = MM(pool.unit(units[n]), lh,
                              ctxT[:, kc * TQ + n * 512: kc * TQ + n * 512 + 512],
                              start=(kc == 0), stop=(kc == NKC - 1),
                              inc="PE" if (kc == NKC - 1 and n == 1) else None)
            d_pe_done[oc] = pe_l
            sch.wait("ac", "PE", pe_l)
            ac_ev = ACT(sumT[:, oc * TQ:(oc + 1) * TQ], P[units[0] // 2],
                        AF.Identity, bias=bcol("bo", oc))
            pool.release(units, "AC", ac_ev)
        A.release()
        tap_dump("sumT", sumT)
        ckstop("D")

        sch.wait("gp", "PE")
        sch.wait("gp", "AC")
        A.mark()
        wg_r = [A.bf(NKC * 128), A.bf(NKC * 128)]
        wu_r = [A.bf(NKC * 128), A.bf(NKC * 128)]
        silu_scr = A.f32(TQ)
        hdn = A.bf(NIC * TQ)

        def hdnc(ic):
            return hdn[:, ic * TQ:(ic + 1) * TQ]

        e_pe_done = {}
        e_dv_done = {}
        for ic in range(NIC):
            er = ic % 2
            if ic >= 2:
                sch.wait("gp", "PE", e_pe_done[("g", ic - 2)])
            wt = DMA("gp", r3(wg_r[er], 128), wslice(wg_d, ic * 128, 128, NKC), "WT")
            units = (0, 1) if ic % 2 == 0 else (4, 5)
            uu = (2, 3) if ic % 2 == 0 else (6, 7)
            pool.acquire(units)
            sch.wait("pe", "WT", wt)
            for kc in range(NKC):
                lh = wg_r[er][:, kc * 128:(kc + 1) * 128]
                for n in range(2):
                    pe_g = MM(pool.unit(units[n]), lh,
                              xnt(kc)[:, n * 512:(n + 1) * 512],
                              start=(kc == 0), stop=(kc == NKC - 1),
                              inc="PE" if (kc == NKC - 1 and n == 1) else None)
            e_pe_done[("g", ic)] = pe_g
            if ic >= 2:
                sch.wait("gp", "PE", e_pe_done[("u", ic - 2)])
            wt = DMA("gp", r3(wu_r[er], 128), wslice(wu_d, ic * 128, 128, NKC), "WT")
            pool.acquire(uu)
            sch.wait("pe", "WT", wt)
            for kc in range(NKC):
                lh = wu_r[er][:, kc * 128:(kc + 1) * 128]
                for n in range(2):
                    pe_u = MM(pool.unit(uu[n]), lh,
                              xnt(kc)[:, n * 512:(n + 1) * 512],
                              start=(kc == 0), stop=(kc == NKC - 1),
                              inc="PE" if (kc == NKC - 1 and n == 1) else None)
            e_pe_done[("u", ic)] = pe_u
            sch.wait("ac", "PE", pe_g)
            if ic >= 1:
                sch.wait("ac", "DV", e_dv_done[ic - 1])
            ac_si = ACT(silu_scr, P[units[0] // 2], AF.Silu,
                        scale=bcol("gs", ic))
            sch.wait("dv", "PE", pe_u)
            sch.wait("dv", "AC", ac_si)
            dv_h = DV("scalar_tensor_tensor", hdnc(ic), P[uu[0] // 2],
                      bcol("us", ic), silu_scr, ALU.mult, ALU.mult)
            e_dv_done[ic] = dv_h
            pool.release(units, "AC", ac_si)
            pool.release(uu, "DV", dv_h)
        sch.barrier()
        wd_r = [A.at_bf(xnT_off, NIC * 128),
                A.at_bf(xnT_off + NIC * 128 + 64, NIC * 128)]
        dn_pe_done = {}
        for oc in range(NKC):
            r = oc % 2
            if oc >= 2:
                sch.wait("gp", "PE", dn_pe_done[oc - 2])
            wt = DMA("gp", r3(wd_r[r], 128), wslice(wd_d, oc * 128, 128, NIC), "WT")
            units = (0, 1) if r == 0 else (2, 3)
            pool.acquire(units)
            sch.wait("pe", "WT", wt)
            for ic in range(NIC):
                lh = wd_r[r][:, ic * 128:(ic + 1) * 128]
                for n in range(2):
                    pe_l = MM(pool.unit(units[n]), lh,
                              hdnc(ic)[:, n * 512:(n + 1) * 512],
                              start=(ic == 0), stop=(ic == NIC - 1),
                              inc="PE" if (ic == NIC - 1 and n == 1) else None)
            dn_pe_done[oc] = pe_l
            sch.wait("dv", "PE", pe_l)
            dv_ev = DV("scalar_tensor_tensor", sumT[:, oc * TQ:(oc + 1) * TQ],
                       P[units[0] // 2], bcol("ds", oc),
                       sumT[:, oc * TQ:(oc + 1) * TQ], ALU.mult, ALU.add)
            pool.release(units, "DV", dv_ev)
        A.release()
        tap_dump("sumT2", sumT)
        ckstop("E")

        sch.barrier()
        A.mark()
        FR = 4
        x_rt = [A.f32(D) for _ in range(FR)]
        stage = [A.f32(D) for _ in range(FR)]
        f_dv_done = {}
        f_st = {}
        for tb in range(8):
            r = tb % FR
            if tb >= FR:
                sch.wait("gp", "DV", f_dv_done[tb - FR])
            ldx = DMA("gp", x_rt[r], x_d[tb * 128:(tb + 1) * 128, :], "WT")
            units = (0, 1, 2, 3) if tb % 2 == 0 else (4, 5, 6, 7)
            pool.acquire(units)
            for oc in range(NKC):
                u = units[oc // 4]
                dst = pool.unit(u)[:, (oc % 4) * 128:(oc % 4) * 128 + 128]
                pe_l = MM(dst,
                          sumT[:, oc * TQ + tb * 128: oc * TQ + tb * 128 + 128],
                          ident, start=(oc % 4 == 0), stop=(oc % 4 == 3),
                          inc="PE" if oc == NKC - 1 else None)
            sch.wait("dv", "PE", pe_l)
            sch.wait("dv", "WT", ldx)
            if tb >= FR:
                sch.wait("dv", "LD", f_st[tb - FR])
            for half in range(2):
                dv_f = DV("tensor_tensor",
                          stage[r].rearrange("p (h c) -> p h c", h=2)[:, half, :],
                          P[units[2 * half] // 2],
                          x_rt[r].rearrange("p (h c) -> p h c", h=2)[:, half, :],
                          ALU.add)
            f_dv_done[tb] = dv_f
            pool.release(units, "DV", dv_f)
            sch.wait("sy", "DV", dv_f)
            f_st[tb] = DMA("sy", out_d[tb * 128:(tb + 1) * 128, :], stage[r],
                           "LD")
        A.release()
        sch.wait("sy", "LD")
        sch.wait("sy", "WT")

        if _trunc:
            sch.ops = sch.ops[:_trunc["n"]]
            if _trunc["ld"]:
                sch.ops.append(("sy", "wait", "LD", _trunc["ld"]))
            if _trunc["wt"]:
                sch.ops.append(("sy", "wait", "WT", _trunc["wt"]))
        block = e(nc.Block())

        def run_for(key):
            pad_nop = key == "dv"

            def prog(eng):
                prev_was_op = False
                for op in sch.ops:
                    if op[0] != key:
                        continue
                    if op[1] == "wait":
                        eng.wait_ge(sems[op[2]], op[3])
                        prev_was_op = False
                    else:
                        if pad_nop and prev_was_op:
                            eng.engine_nop()
                        inst = op[2](eng)
                        if op[3] is not None:
                            inst.then_inc(sems[op[3][0]], op[3][1])
                        prev_was_op = True
            return prog
        block.tensor(run_for("pe"))
        block.vector(run_for("dv"))
        block.scalar(run_for("ac"))
        block.sync(run_for("sy"))
        block.gpsimd(run_for("gp"))

    return nc



def _host_prep(inputs):
    x_full = np.asarray(inputs["L0_attn_h"], np.float32)
    shared = {
        "wdkv": np.ascontiguousarray(np.asarray(inputs["L0_attn_W_DKV"], np.float32)),
        "wdq": np.ascontiguousarray(np.asarray(inputs["L0_attn_W_DQ"], np.float32)),
        "wuk": np.ascontiguousarray(np.asarray(inputs["L0_attn_W_UK"], np.float32)),
        "wuv": np.ascontiguousarray(np.asarray(inputs["L0_attn_W_UV"], np.float32)),
        "wuq": np.ascontiguousarray(np.asarray(inputs["L0_attn_W_UQ"], np.float32)),
        "wkr": np.ascontiguousarray(np.asarray(inputs["L0_attn_W_KR"], np.float32)),
        "wo": np.ascontiguousarray(np.asarray(inputs["L0_attn_W_O"], np.float32)),
        "wg": np.ascontiguousarray(np.asarray(inputs["L0_ffn_gate_w"], np.float32).T),
        "wu": np.ascontiguousarray(np.asarray(inputs["L0_ffn_up_w"], np.float32).T),
        "wd": np.ascontiguousarray(np.asarray(inputs["L0_ffn_down_w"], np.float32).T),
        "ident": np.eye(128, dtype=np.float32).astype(BF),
        "buvrow": np.asarray(inputs["L0_attn_b_UV"], np.float32).astype(BF).reshape(1, D),
    }
    biasT = np.zeros((128, NBIAS), np.float32)
    def put(name, vec):
        vec = np.asarray(vec, np.float32)
        n = vec.size // 128
        biasT[:, _BCOLS[name]:_BCOLS[name] + n] = vec.reshape(n, 128).T
    put("bdkv", inputs["L0_attn_b_DKV"])
    put("bdq", inputs["L0_attn_b_DQ"])
    put("buk", inputs["L0_attn_b_UK"])
    put("buq", inputs["L0_attn_b_UQ"])
    put("bkr", inputs["L0_attn_b_KR"])
    put("bo", inputs["L0_attn_b_O"])
    put("gs", inputs["L0_ffn_gate_s"])
    put("us", inputs["L0_ffn_up_s"])
    put("ds", inputs["L0_ffn_down_s"])
    biasT[:, _BCOLS["negb"]] = NEGB
    biasT[:, _BCOLS["eps"]] = 1e-5
    shared["biasT"] = biasT
    rr = np.arange(128)
    shared["tri"] = np.where(rr[:, None] <= rr[None, :], 0.0,
                             -29952.0).astype(np.float32).astype(BF)

    in_maps = []
    for c in range(8):
        b, p = c // 2, c % 2
        ob = [2 * j + p for j in range(8)] + [2 * j + (1 - p) for j in range(8)]
        perm = np.concatenate([np.arange(o * 128, (o + 1) * 128) for o in ob])
        m = dict(shared)
        m["x"] = np.ascontiguousarray(x_full[b][perm])
        m["pmask"] = np.full((128, 128), 0.0 if p == 1 else -29952.0,
                             np.float32).astype(BF)
        pos = perm.astype(np.float64)
        inv = 1.0 / (10000.0 ** (np.arange(64, dtype=np.float64) * 2.0 / 128.0))
        ang = inv[:, None] * pos[None, :]
        c2 = np.repeat(np.cos(ang), 2, axis=0)
        s2 = np.repeat(np.sin(ang), 2, axis=0)
        s2[0::2] *= -1.0
        m["c2k"] = c2.astype(np.float32).astype(BF)
        m["s2k"] = s2.astype(np.float32).astype(BF)
        in_maps.append(m)
    return in_maps


def kernel(**inputs):
    in_maps = _host_prep(inputs)
    nc = build_program()
    res = run_bass_kernel_spmd(nc, in_maps, core_ids=list(range(8)))
    out = np.empty((4, S, D), np.float32)
    for c in range(8):
        b, p = c // 2, c % 2
        r = np.asarray(res.results[c]["out"], np.float32)
        for j in range(8):
            o = 2 * j + p
            out[b, o * 128:(o + 1) * 128, :] = r[j * 128:(j + 1) * 128, :]
    return out


# version 13
# speedup vs baseline: 1.0241x; 1.0241x over previous
import numpy as np
import ml_dtypes
from collections import defaultdict
from contextlib import ExitStack

import concourse.bass as bass
import concourse.mybir as mybir
from concourse.bass_utils import run_bass_kernel_spmd

F32 = mybir.dt.float32
BF16 = mybir.dt.bfloat16
AF = mybir.ActivationFunctionType
ALU = mybir.AluOpType
AXX = mybir.AxisListType.X

BF = ml_dtypes.bfloat16

S = 2048
D = 2048
H = 16
HD = 128
DKV = 512
DQ = 1536
INTER = 5632
TQ = 1024
NKC = D // 128
NIC = INTER // 128
NEGB = -15.0
SCALE = 1.0 / 16.0
SHUF_SWAP = [i ^ 1 for i in range(32)]

_BCOLS = {}
_nb = 0
def _bc(name, n):
    global _nb
    _BCOLS[name] = _nb
    _nb += n
_bc("bdkv", 4); _bc("bdq", 12); _bc("buk", 16); _bc("buq", 16); _bc("bkr", 16)
_bc("bo", 16); _bc("gs", 44); _bc("us", 44); _bc("ds", 16); _bc("negb", 1)
_bc("eps", 1)
NBIAS = _nb


class Sched:
    ENGS = ("pe", "dv", "ac", "sy", "gp")

    def __init__(self):
        self.ops = []
        self.cnt = defaultdict(int)

    def do(self, eng, fn, inc=None, n=1):
        assert eng in self.ENGS
        self.ops.append((eng, "op", fn, (inc, n) if inc else None))
        if inc:
            self.cnt[inc] += n
            return self.cnt[inc]
        return None

    def wait(self, eng, sem, val=None):
        if val is None:
            val = self.cnt[sem]
        if val is None or val <= 0:
            return
        self.ops.append((eng, "wait", sem, val))

    def now(self, sem):
        return self.cnt[sem]

    def barrier(self):
        snap = {s: self.cnt[s] for s in ("PE", "DV", "AC", "LD", "WT")}
        for e in self.ENGS:
            for s_, v in snap.items():
                self.wait(e, s_, v)


class Arena:

    def __init__(self, handle, total):
        self.ap_full = handle.ap()
        self.total = total
        self.off = 0
        self.marks = []

    def bf(self, n):
        n = (n + 63) & ~63
        assert self.off + n <= self.total, f"arena overflow {self.off}+{n}"
        ap = self.ap_full[:, self.off:self.off + n]
        self.off += n
        return ap

    def f32(self, n):
        n2 = 2 * ((n + 31) & ~31)
        assert self.off + n2 <= self.total, f"arena overflow {self.off}+{n2}"
        ap = self.ap_full[:, self.off:self.off + n2].bitcast(F32)
        self.off += n2
        return ap

    def at_bf(self, off, n):
        assert off + n <= self.total
        return self.ap_full[:, off:off + n]

    def mark(self):
        self.marks.append(self.off)

    def release(self):
        self.off = self.marks.pop()


class PsumPool:

    def __init__(self, sched, tensors):
        self.s = sched
        self.P = tensors
        self.free_at = [None] * 8

    def unit(self, u):
        return self.P[u // 2][:, (u % 2) * 512:(u % 2) * 512 + 512]

    def acquire(self, units):
        for u in units:
            fa = self.free_at[u]
            if fa:
                self.s.wait("pe", fa[0], fa[1])
                self.free_at[u] = None

    def release(self, units, sem, cnt):
        for u in units:
            self.free_at[u] = (sem, cnt)


def seg512(a, b):
    out = []
    while a < b:
        e = min((a // 512 + 1) * 512, b)
        out.append((a, e))
        a = e
    return out


class _Stop(Exception):
    pass


def build_program(taps=(), stop_after=None):
    nc = bass.Bass("TRN2", target_bir_lowering=False)
    taps = set(taps)

    x_d = nc.dram_tensor("x", [S, D], F32, kind="ExternalInput").ap()
    wdkv_d = nc.dram_tensor("wdkv", [D, DKV], F32, kind="ExternalInput").ap()
    wdq_d = nc.dram_tensor("wdq", [D, DQ], F32, kind="ExternalInput").ap()
    wuk_d = nc.dram_tensor("wuk", [DKV, D], F32, kind="ExternalInput").ap()
    wuv_d = nc.dram_tensor("wuv", [DKV, D], F32, kind="ExternalInput").ap()
    wuq_d = nc.dram_tensor("wuq", [DQ, D], F32, kind="ExternalInput").ap()
    wkr_d = nc.dram_tensor("wkr", [D, D], F32, kind="ExternalInput").ap()
    wo_d = nc.dram_tensor("wo", [D, D], F32, kind="ExternalInput").ap()
    wg_d = nc.dram_tensor("wg", [D, INTER], F32, kind="ExternalInput").ap()
    wu_d = nc.dram_tensor("wu", [D, INTER], F32, kind="ExternalInput").ap()
    wd_d = nc.dram_tensor("wd", [INTER, D], F32, kind="ExternalInput").ap()
    biasT_d = nc.dram_tensor("biasT", [128, NBIAS], F32, kind="ExternalInput").ap()
    ident_d = nc.dram_tensor("ident", [128, 128], BF16, kind="ExternalInput").ap()
    tri_d = nc.dram_tensor("tri", [128, 128], BF16, kind="ExternalInput").ap()
    pmask_d = nc.dram_tensor("pmask", [128, 128], BF16, kind="ExternalInput").ap()
    c2k_d = nc.dram_tensor("c2k", [128, S], BF16, kind="ExternalInput").ap()
    s2k_d = nc.dram_tensor("s2k", [128, S], BF16, kind="ExternalInput").ap()
    buv_d = nc.dram_tensor("buvrow", [1, D], BF16, kind="ExternalInput").ap()
    ctx_hbm = nc.dram_tensor(
        "ctxscr", [H * HD, TQ], BF16,
        kind=("ExternalOutput" if "ctx" in taps else "Internal")).ap()
    out_d = nc.dram_tensor("out", [TQ, D], F32, kind="ExternalOutput").ap()
    lscr_d = nc.dram_tensor("lscr", [1, TQ], F32, kind="Internal").ap()

    tap_d = {}
    def decl_tap(name, shape, dt):
        if name in taps:
            tap_d[name] = nc.dram_tensor("tap_" + name, shape, dt,
                                         kind="ExternalOutput").ap()
    decl_tap("xnT", [128, NKC * S], BF16)
    decl_tap("ckvT", [128, 4 * S], BF16)
    decl_tap("cqT", [128, 12 * TQ], BF16)
    decl_tap("kCT", [128, S], BF16)
    decl_tap("kRT", [128, S], BF16)
    decl_tap("vG", [128, 16 * 512], BF16)
    decl_tap("qCT", [128, TQ], BF16)
    decl_tap("qRT", [128, TQ], BF16)
    decl_tap("sumT", [128, NKC * TQ], BF16)
    decl_tap("sumT2", [128, NKC * TQ], BF16)

    sch = Sched()

    with ExitStack() as es:
        e = es.enter_context
        ARENA_N = 106368
        arena_h = e(nc.sbuf_tensor("arena", [128, ARENA_N], BF16))
        A = Arena(arena_h, ARENA_N)
        P = [e(nc.psum_tensor(f"psum{i}", [128, 1024], F32)).ap()
             for i in range(4)]
        pool = PsumPool(sch, P)
        sems = {s_: e(nc.semaphore("sem" + s_))
                for s_ in ("LD", "WT", "PE", "DV", "AC")}

        biases = A.f32(NBIAS)
        ident = A.bf(128)
        ones_bf = A.bf(64)[:, 0:1]
        ones1_bf = A.bf(128)[0:1, :]
        ones1_f = A.f32(128)[0:1, :]
        xnT_off = A.off
        xnT = A.bf(NKC * S)

        def xnt(kc):
            return xnT[:, kc * S:(kc + 1) * S]

        def bcol(name, i=0):
            c = _BCOLS[name] + i
            return biases[:, c:c + 1]

        def MM(out, lh, rh, start, stop, inc=None):
            return sch.do("pe", lambda t, o=out, l=lh, r=rh, s1=start, s2=stop:
                          t.matmul(o, l, r, start=s1, stop=s2), inc)

        def ACT(out, in_, func, bias=0.0, scale=1.0, accum=None, inc="AC"):
            return sch.do("ac", lambda a, o=out, i=in_, f=func, b=bias, s=scale,
                          ac_=accum: a.activation(out=o, in_=i, func=f, bias=b,
                                                  scale=s, accum_out=ac_), inc)

        def DV(meth, *args, inc="DV", **kw):
            def fn(v, m=meth, a=args, k=dict(kw)):
                return getattr(v, m)(*a, **k)
            return sch.do("dv", fn, inc)

        def DMA(q, out, in_, sem):
            return sch.do(q, lambda g, o=out, i=in_: g.dma_start(out=o, in_=i),
                          sem, 16)

        def tap_dump(name, src_ap):
            if name in tap_d:
                sch.barrier()
                DMA("sy", tap_d[name], src_ap, "LD")

        def wslice(dram, col0, ncols, nch, p=128):
            return dram[:, col0:col0 + ncols].rearrange(
                "(kc p) c -> p kc c", p=p)

        def r3(sbuf_ap, ncols):
            return sbuf_ap.rearrange("p (kc c) -> p kc c", c=ncols)

        _trunc = {}

        def ckstop(ph):
            if stop_after == ph and not _trunc:
                _trunc["n"] = len(sch.ops)
                _trunc["ld"] = sch.cnt["LD"]
                _trunc["wt"] = sch.cnt["WT"]

        DMA("sy", biases[:, 0:NBIAS], biasT_d, "LD")
        DMA("sy", ident, ident_d, "LD")
        setup_ld = sch.now("LD")
        DV("memset", ones_bf, 1.0)
        DV("memset", ones1_bf, 1.0)
        DV("memset", ones1_f, 1.0)

        A.mark()
        tri = A.bf(128)
        pmask = A.bf(128)
        c2k = A.bf(S)
        s2k = A.bf(S)
        buv_row = A.bf(D)[0:1, :]
        ckvT = A.bf(4 * S)
        cqT = A.bf(12 * TQ)
        DMA("sy", tri, tri_d, "LD")
        DMA("sy", pmask, pmask_d, "LD")
        DMA("sy", c2k, c2k_d, "LD")
        DMA("sy", s2k, s2k_d, "LD")
        DMA("sy", buv_row, buv_d, "LD")
        tables_ld = sch.now("LD")

        A.mark()
        AR = 4
        x_t = [A.f32(D) for _ in range(AR)]
        xn_tm = [A.bf(D) for _ in range(AR)]
        sfull = A.f32(32 * AR)
        st6 = [sfull[:, 32 * i:32 * i + 24] for i in range(AR)]
        st2c = [sfull[:, 32 * i + 24:32 * i + 26] for i in range(AR)]
        wdkv_all = A.bf(4 * NKC * 128)
        wdq_r = [A.bf(NKC * 128) for _ in range(4)]

        a_dv_done = {}
        a_ac_done = {}
        a_pe_done = {}
        sch.wait("dv", "LD", setup_ld)
        sch.wait("pe", "LD", setup_ld)
        sch.wait("ac", "LD", setup_ld)
        for oc in range(4):
            wt = DMA("gp", r3(wdkv_all[:, oc * NKC * 128:(oc + 1) * NKC * 128],
                              128),
                     wslice(wdkv_d, oc * 128, 128, NKC), "WT")
        wdkv_wt = sch.now("WT")
        wdq_wt = {}
        wdq_pe = {}

        def emit_tile_head(t):
            r = t % AR
            if t >= AR:
                sch.wait("sy", "DV", a_dv_done[t - AR])
            ld = DMA("sy", x_t[r], x_d[t * 128:(t + 1) * 128, :], "LD")
            sch.wait("dv", "LD", ld)
            for j in range(4):
                DV("bn_stats", st6[r][:, 6 * j:6 * j + 6],
                   x_t[r][:, 512 * j:512 * j + 512])
            dv_agg = DV("bn_aggr", st2c[r], st6[r])
            sch.wait("ac", "DV", dv_agg)
            ac_ln = ACT(st2c[r][:, 1:2], st2c[r][:, 1:2], AF.Ln,
                        bias=bcol("eps"))
            sch.wait("ac", "AC", ac_ln)
            ac_rstd = ACT(st2c[r][:, 1:2], st2c[r][:, 1:2], AF.Exp,
                          scale=-0.5)
            sch.wait("dv", "AC", ac_rstd)
            if t >= AR:
                sch.wait("dv", "PE", a_pe_done[t - AR])
            dv_app = DV("tensor_scalar", xn_tm[r], x_t[r], st2c[r][:, 0:1],
                        st2c[r][:, 1:2], ALU.subtract, ALU.mult)
            a_dv_done[t] = dv_app

        def emit_tile_tail(t):
            r = t % AR
            sch.wait("pe", "DV", a_dv_done[t])
            for half in range(2):
                units = (0, 1) if half == 0 else (2, 3)
                pool.acquire(units)
                for j in range(8):
                    kc = 8 * half + j
                    u = units[j // 4]
                    dst = pool.unit(u)[:, (j % 4) * 128:(j % 4) * 128 + 128]
                    pe_last = MM(dst, xn_tm[r][:, kc * 128:(kc + 1) * 128],
                                 ident, start=(j % 4 == 0), stop=(j % 4 == 3),
                                 inc="PE" if j == 7 else None)
                if half == 1:
                    a_pe_done[t] = pe_last
                sch.wait("ac", "PE", pe_last)
                pv = P[0] if half == 0 else P[1]
                src_ = pv.rearrange("p (kc c) -> p kc c", c=128)
                dst_ = xnT.rearrange("p (kc tt) -> p kc tt", tt=S)[
                    :, 8 * half:8 * half + 8, t * 128:(t + 1) * 128]
                ac_ev = ACT(dst_, src_, AF.Identity)
                pool.release(units, "AC", ac_ev)
            a_ac_done[t] = ac_ev

        def emit_wdq_dma(n, oc):
            prev = wdq_pe.get((n, oc - 4)) if oc >= 4 else (
                wdq_pe.get((0, oc + 8)) if n == 1 else None)
            if prev:
                sch.wait("gp", "PE", prev)
            wdq_wt[(n, oc)] = DMA("gp", r3(wdq_r[oc % 4], 128),
                                  wslice(wdq_d, oc * 128, 128, NKC), "WT")

        def emit_bsec(n):
            sch.wait("pe", "AC", a_ac_done[4 * n + 3])
            if n == 0:
                sch.wait("pe", "WT", wdkv_wt)
            chunks = [("kv", oc) for oc in range(4)]
            if n < 2:
                chunks += [("q", oc) for oc in range(12)]
                for oc in range(4):
                    emit_wdq_dma(n, oc)
            for k, (kind, oc) in enumerate(chunks):
                u = 4 + k % 4
                pool.acquire((u,))
                if kind == "q":
                    sch.wait("pe", "WT", wdq_wt[(n, oc)])
                for kc in range(NKC):
                    lh = (wdkv_all[:, (oc * NKC + kc) * 128:
                                   (oc * NKC + kc) * 128 + 128]
                          if kind == "kv" else
                          wdq_r[oc % 4][:, kc * 128:(kc + 1) * 128])
                    pe_l = MM(pool.unit(u), lh,
                              xnt(kc)[:, n * 512:(n + 1) * 512],
                              start=(kc == 0), stop=(kc == NKC - 1),
                              inc="PE" if kc == NKC - 1 else None)
                if kind == "q":
                    wdq_pe[(n, oc)] = pe_l
                    if oc + 4 < 12:
                        emit_wdq_dma(n, oc + 4)
                sch.wait("ac", "PE", pe_l)
                if kind == "kv":
                    dst = ckvT[:, oc * S + n * 512: oc * S + (n + 1) * 512]
                    b = bcol("bdkv", oc)
                else:
                    dst = cqT[:, oc * TQ + n * 512: oc * TQ + (n + 1) * 512]
                    b = bcol("bdq", oc)
                ac_ev = ACT(dst, pool.unit(u), AF.Identity, bias=b)
                pool.release((u,), "AC", ac_ev)

        for t in range(4):
            emit_tile_head(t)
            emit_tile_tail(t)
        for n in range(4):
            for t in range(4 * n + 4, min(4 * n + 8, 16)):
                emit_tile_head(t)
            emit_bsec(n)
            for t in range(4 * n + 4, min(4 * n + 8, 16)):
                emit_tile_tail(t)
        A.release()
        tap_dump("xnT", xnT)
        tap_dump("ckvT", ckvT)
        tap_dump("cqT", cqT)
        ckstop("B")

        sch.barrier()
        A.mark()
        kRT = [A.bf(S), A.bf(S)]
        kCT = [A.bf(S), A.bf(S)]
        vG = A.bf(16 * 512)
        qCT = [A.bf(TQ), A.bf(TQ)]
        qRT = [A.bf(TQ), A.bf(TQ)]
        swp = A.bf(S)
        Et = [A.bf(TQ), A.bf(TQ)]
        linv_rep = A.f32(TQ)
        ctx_h = A.bf(TQ)
        ctx_n = A.bf(TQ)
        wkr_r = [A.bf(NKC * 128), A.bf(NKC * 128)]
        wuk_r = [A.bf(4 * 128), A.bf(4 * 128)]
        wuv_r = A.bf(4 * 512)
        wuq_r = [A.bf(12 * 128), A.bf(12 * 128)]

        sch.wait("dv", "LD", tables_ld)
        sch.wait("pe", "LD", tables_ld)

        v_pe_done = {}
        att_end_pe = {}
        kr_pe_d = {}
        kc_pe_d = {}
        q_pe_d = {}
        dv_qrope_d = {}
        ac_kc_d = {}
        dv_ctx_d = {}
        lout_ld = {}
        lbcast_ld = {}
        ctx_store = {}

        def emit_vgroup(g):
            if g > 0:
                sch.wait("gp", "PE", v_pe_done[g - 1])
            wt = DMA("gp", r3(wuv_r, 512), wslice(wuv_d, g * 512, 512, 4),
                     "WT")
            sch.wait("pe", "WT", wt)
            if g > 0:
                sch.wait("ac", "PE", att_end_pe[4 * g - 1])
            uperm = (0, 1, 2, 3, 6, 7, 4, 5)
            for kb in range(16):
                u = uperm[kb % 8]
                pool.acquire((u,))
                for kc in range(4):
                    MM(pool.unit(u),
                       ckvT[:, kc * S + kb * 128: kc * S + kb * 128 + 128],
                       wuv_r[:, kc * 512:(kc + 1) * 512],
                       start=(kc == 0), stop=False)
                pe_l = MM(pool.unit(u), ones1_bf,
                          buv_row[:, g * 512:(g + 1) * 512],
                          start=False, stop=True, inc="PE")
                sch.wait("ac", "PE", pe_l)
                ac_ev = ACT(vG[:, kb * 512:(kb + 1) * 512], pool.unit(u),
                            AF.Identity)
                pool.release((u,), "AC", ac_ev)
            v_pe_done[g] = sch.now("PE")
            if g == 0:
                tap_dump("vG", vG)

        def emit_prod(hh):
            r = hh % 2
            if hh >= 2:
                sch.wait("ac", "PE", att_end_pe[hh - 2])
                sch.wait("dv", "PE", att_end_pe[hh - 2])
                sch.wait("gp", "PE", kr_pe_d[hh - 2])
            wt = DMA("gp", r3(wkr_r[r], 128),
                     wslice(wkr_d, hh * 128, 128, NKC), "WT")
            units = (0, 1, 2, 3)
            pool.acquire(units)
            sch.wait("pe", "WT", wt)
            for kc in range(NKC):
                lh = wkr_r[r][:, kc * 128:(kc + 1) * 128]
                for n in range(4):
                    pe_l = MM(pool.unit(units[n]),
                              lh, xnt(kc)[:, n * 512:(n + 1) * 512],
                              start=(kc == 0), stop=(kc == NKC - 1),
                              inc="PE" if (kc == NKC - 1 and n == 3) else None)
            kr_pe_d[hh] = pe_l
            sch.wait("ac", "PE", pe_l)
            for half in range(2):
                ac_ev = ACT(kRT[r][:, half * 1024:half * 1024 + 1024],
                            P[units[2 * half] // 2], AF.Identity,
                            bias=bcol("bkr", hh))
            pool.release(units, "AC", ac_ev)
            sch.wait("dv", "AC", ac_ev)
            DV("stream_shuffle", swp, kRT[r], SHUF_SWAP)
            DV("tensor_tensor", kRT[r], kRT[r], c2k, ALU.mult)
            DV("tensor_tensor", swp, swp, s2k, ALU.mult)
            DV("tensor_tensor", kRT[r], kRT[r], swp, ALU.add)
            if hh >= 2:
                sch.wait("gp", "PE", kc_pe_d[hh - 2])
            wt = DMA("gp", r3(wuk_r[r], 128),
                     wslice(wuk_d, hh * 128, 128, 4), "WT")
            units = (4, 5, 6, 7)
            pool.acquire(units)
            sch.wait("pe", "WT", wt)
            for kc in range(4):
                lh = wuk_r[r][:, kc * 128:(kc + 1) * 128]
                for n in range(4):
                    pe_l = MM(pool.unit(units[n]), lh,
                              ckvT[:, kc * S + n * 512: kc * S + n * 512 + 512],
                              start=(kc == 0), stop=(kc == 3),
                              inc="PE" if (kc == 3 and n == 3) else None)
            kc_pe_d[hh] = pe_l
            sch.wait("ac", "PE", pe_l)
            for half in range(2):
                ac_kc = ACT(kCT[r][:, half * 1024:half * 1024 + 1024],
                            P[units[2 * half] // 2], AF.Identity,
                            bias=bcol("buk", hh))
            pool.release(units, "AC", ac_kc)
            ac_kc_d[hh] = ac_kc
            if hh >= 2:
                sch.wait("gp", "PE", q_pe_d[hh - 2])
            wt = DMA("gp", r3(wuq_r[r], 128),
                     wslice(wuq_d, hh * 128, 128, 12), "WT")
            units = (0, 1)
            pool.acquire(units)
            sch.wait("pe", "WT", wt)
            for kc in range(12):
                lh = wuq_r[r][:, kc * 128:(kc + 1) * 128]
                for n in range(2):
                    pe_l = MM(pool.unit(units[n]), lh,
                              cqT[:, kc * TQ + n * 512: kc * TQ + n * 512 + 512],
                              start=(kc == 0), stop=(kc == 11),
                              inc="PE" if (kc == 11 and n == 1) else None)
            q_pe_d[hh] = pe_l
            sch.wait("ac", "PE", pe_l)
            ac_q = ACT(qCT[r], P[0], AF.Identity, bias=bcol("buq", hh))
            pool.release(units, "AC", ac_q)
            sch.wait("dv", "AC", ac_q)
            DV("stream_shuffle", swp[:, 0:TQ], qCT[r], SHUF_SWAP)
            DV("tensor_tensor", qRT[r], qCT[r], c2k[:, 0:TQ], ALU.mult)
            DV("tensor_tensor", swp[:, 0:TQ], swp[:, 0:TQ], s2k[:, 0:TQ],
               ALU.mult)
            dv_qrope_d[hh] = DV("tensor_tensor", qRT[r], qRT[r],
                                swp[:, 0:TQ], ALU.add)
            if hh == 0:
                tap_dump("kRT", kRT[r])
                tap_dump("kCT", kCT[r])
                tap_dump("qCT", qCT[r])
                tap_dump("qRT", qRT[r])

        def emit_att(hh):
            oc = hh % 4
            r = hh % 2
            pool.acquire((0, 1, 2, 3))
            sch.wait("pe", "DV", dv_qrope_d[hh])
            sch.wait("pe", "AC", ac_kc_d[hh])
            ctxl_acquired = [False]
            exp_done = {}
            ctxl_done = {}
            last_kb_bank = {0: 11, 1: 15}

            def emit_ctxl(kb):
                j0 = kb if kb < 8 else kb - 8
                q0 = j0 * 128
                par = kb % 2
                segs = seg512(q0, TQ)
                if not ctxl_acquired[0]:
                    pool.acquire((4, 5, 6, 7))
                    ctxl_acquired[0] = True
                sch.wait("pe", "AC", exp_done[kb])
                for a_, b_ in segs:
                    bank = a_ // 512
                    MM(P[2][:, a_:b_],
                       vG[:, kb * 512 + oc * 128: kb * 512 + oc * 128 + 128],
                       Et[par][:, a_:b_], start=(kb == 0),
                       stop=(kb == last_kb_bank[bank]))
                for i_, (a_, b_) in enumerate(segs):
                    bank = a_ // 512
                    pe_cl = MM(P[3][0:1, a_:b_], ones_bf, Et[par][:, a_:b_],
                               start=(kb == 0),
                               stop=(kb == last_kb_bank[bank]),
                               inc="PE" if i_ == len(segs) - 1 else None)
                ctxl_done[kb] = pe_cl

            for kb in range(16):
                j0 = kb if kb < 8 else kb - 8
                q0 = j0 * 128
                par = kb % 2
                Spar = P[par]
                if kb >= 2:
                    sch.wait("pe", "AC", exp_done[kb - 2])
                segs = seg512(q0, TQ)
                q0bank = q0 // 512
                mk = tri if kb < 8 else pmask
                MM(Spar[:, q0:q0 + 128], ident, mk, start=True, stop=False)
                for a_, b_ in segs:
                    MM(Spar[:, a_:b_],
                       kCT[r][:, kb * 128:kb * 128 + 128],
                       qCT[r][:, a_:b_],
                       start=(a_ // 512 != q0bank), stop=False)
                for i_, (a_, b_) in enumerate(segs):
                    pe_sc = MM(Spar[:, a_:b_],
                               kRT[r][:, kb * 128:kb * 128 + 128],
                               qRT[r][:, a_:b_], start=False, stop=True,
                               inc="PE" if i_ == len(segs) - 1 else None)
                sch.wait("ac", "PE", pe_sc)
                if kb >= 2:
                    sch.wait("ac", "PE", ctxl_done[kb - 2])
                ac_exp = ACT(Et[par][:, q0:TQ], Spar[:, q0:TQ], AF.Exp,
                             bias=bcol("negb"), scale=SCALE)
                exp_done[kb] = ac_exp
                if kb >= 1:
                    emit_ctxl(kb - 1)
            emit_ctxl(15)
            sch.wait("ac", "PE", ctxl_done[15])
            if hh >= 1:
                sch.wait("ac", "DV", dv_ctx_d[hh - 1])
            ac_cu = ACT(ctx_h, P[2], AF.Identity)
            if hh >= 1:
                sch.wait("ac", "LD", lbcast_ld[hh - 1])
            ac_ln = ACT(linv_rep[0:1, :], P[3][0:1, :], AF.Ln)
            sch.wait("ac", "AC", ac_ln)
            ac_ex = ACT(linv_rep[0:1, :], linv_rep[0:1, :], AF.Exp,
                        scale=-1.0)
            if hh >= 1:
                sch.wait("sy", "LD", lbcast_ld[hh - 1])
            sch.wait("sy", "AC", ac_ex)
            lout_ld[hh] = DMA("sy", lscr_d, linv_rep[0:1, :], "LD")
            if hh >= 1:
                sch.wait("sy", "DV", dv_ctx_d[hh - 1])
            sch.wait("sy", "LD", lout_ld[hh])
            lbcast_ld[hh] = DMA("sy", linv_rep,
                                lscr_d.broadcast_to([128, TQ]), "LD")
            sch.wait("dv", "LD", lbcast_ld[hh])
            if hh >= 1:
                sch.wait("dv", "LD", ctx_store[hh - 1])
            sch.wait("dv", "AC", ac_cu)
            dv_ctx = DV("tensor_tensor", ctx_n, ctx_h, linv_rep, ALU.mult)
            dv_ctx_d[hh] = dv_ctx
            sch.wait("sy", "DV", dv_ctx)
            ctx_store[hh] = DMA("sy", ctx_hbm[hh * 128:(hh + 1) * 128, :],
                                ctx_n, "LD")
            att_end_pe[hh] = ctxl_done[15]
            pool.release((0, 1), "AC", exp_done[14])
            pool.release((2, 3), "AC", exp_done[15])
            pool.release((4, 5), "AC", ac_cu)
            pool.release((6, 7), "AC", ac_ln)

        for g in range(4):
            emit_vgroup(g)
            for oc in range(4):
                hh = 4 * g + oc
                emit_prod(hh)
                if oc >= 1:
                    emit_att(hh - 1)
            emit_att(4 * g + 3)
        A.release()
        A.release()
        ckstop("C")

        sch.wait("gp", "PE")
        sch.wait("gp", "DV")
        sch.wait("sy", "DV")
        sch.wait("sy", "AC")
        sch.wait("sy", "PE")
        sumT = A.bf(NKC * TQ)
        A.mark()
        ctxT = A.bf(NKC * TQ)
        wo_r = [A.bf(NKC * 128), A.bf(NKC * 128)]
        for kc in range(NKC):
            DMA("sy", ctxT[:, kc * TQ:(kc + 1) * TQ],
                ctx_hbm[kc * 128:(kc + 1) * 128, :], "LD")
        ctx_ld = sch.now("LD")
        sch.wait("pe", "LD", ctx_ld)
        d_pe_done = {}
        for oc in range(NKC):
            r = oc % 2
            if oc >= 2:
                sch.wait("gp", "PE", d_pe_done[oc - 2])
            wt = DMA("gp", r3(wo_r[r], 128), wslice(wo_d, oc * 128, 128, NKC), "WT")
            units = (0, 1) if r == 0 else (2, 3)
            pool.acquire(units)
            sch.wait("pe", "WT", wt)
            for kc in range(NKC):
                lh = wo_r[r][:, kc * 128:(kc + 1) * 128]
                for n in range(2):
                    pe_l = MM(pool.unit(units[n]), lh,
                              ctxT[:, kc * TQ + n * 512: kc * TQ + n * 512 + 512],
                              start=(kc == 0), stop=(kc == NKC - 1),
                              inc="PE" if (kc == NKC - 1 and n == 1) else None)
            d_pe_done[oc] = pe_l
            sch.wait("ac", "PE", pe_l)
            ac_ev = ACT(sumT[:, oc * TQ:(oc + 1) * TQ], P[units[0] // 2],
                        AF.Identity, bias=bcol("bo", oc))
            pool.release(units, "AC", ac_ev)
        A.release()
        tap_dump("sumT", sumT)
        ckstop("D")

        sch.wait("gp", "PE")
        sch.wait("gp", "AC")
        A.mark()
        wg_r = [A.bf(NKC * 128), A.bf(NKC * 128)]
        wu_r = [A.bf(NKC * 128), A.bf(NKC * 128)]
        silu_scr = A.f32(TQ)
        hdn = A.bf(NIC * TQ)

        def hdnc(ic):
            return hdn[:, ic * TQ:(ic + 1) * TQ]

        e_pe_done = {}
        e_dv_done = {}
        for ic in range(NIC):
            er = ic % 2
            if ic >= 2:
                sch.wait("gp", "PE", e_pe_done[("g", ic - 2)])
            wt = DMA("gp", r3(wg_r[er], 128), wslice(wg_d, ic * 128, 128, NKC), "WT")
            units = (0, 1) if ic % 2 == 0 else (4, 5)
            uu = (2, 3) if ic % 2 == 0 else (6, 7)
            pool.acquire(units)
            sch.wait("pe", "WT", wt)
            for kc in range(NKC):
                lh = wg_r[er][:, kc * 128:(kc + 1) * 128]
                for n in range(2):
                    pe_g = MM(pool.unit(units[n]), lh,
                              xnt(kc)[:, n * 512:(n + 1) * 512],
                              start=(kc == 0), stop=(kc == NKC - 1),
                              inc="PE" if (kc == NKC - 1 and n == 1) else None)
            e_pe_done[("g", ic)] = pe_g
            if ic >= 2:
                sch.wait("gp", "PE", e_pe_done[("u", ic - 2)])
            wt = DMA("gp", r3(wu_r[er], 128), wslice(wu_d, ic * 128, 128, NKC), "WT")
            pool.acquire(uu)
            sch.wait("pe", "WT", wt)
            for kc in range(NKC):
                lh = wu_r[er][:, kc * 128:(kc + 1) * 128]
                for n in range(2):
                    pe_u = MM(pool.unit(uu[n]), lh,
                              xnt(kc)[:, n * 512:(n + 1) * 512],
                              start=(kc == 0), stop=(kc == NKC - 1),
                              inc="PE" if (kc == NKC - 1 and n == 1) else None)
            e_pe_done[("u", ic)] = pe_u
            sch.wait("ac", "PE", pe_g)
            if ic >= 1:
                sch.wait("ac", "DV", e_dv_done[ic - 1])
            ac_si = ACT(silu_scr, P[units[0] // 2], AF.Silu,
                        scale=bcol("gs", ic))
            sch.wait("dv", "PE", pe_u)
            sch.wait("dv", "AC", ac_si)
            dv_h = DV("scalar_tensor_tensor", hdnc(ic), P[uu[0] // 2],
                      bcol("us", ic), silu_scr, ALU.mult, ALU.mult)
            e_dv_done[ic] = dv_h
            pool.release(units, "AC", ac_si)
            pool.release(uu, "DV", dv_h)
        sch.barrier()
        wd_r = [A.at_bf(xnT_off, NIC * 128),
                A.at_bf(xnT_off + NIC * 128 + 64, NIC * 128)]
        dn_pe_done = {}
        for oc in range(NKC):
            r = oc % 2
            if oc >= 2:
                sch.wait("gp", "PE", dn_pe_done[oc - 2])
            wt = DMA("gp", r3(wd_r[r], 128), wslice(wd_d, oc * 128, 128, NIC), "WT")
            units = (0, 1) if r == 0 else (2, 3)
            pool.acquire(units)
            sch.wait("pe", "WT", wt)
            for ic in range(NIC):
                lh = wd_r[r][:, ic * 128:(ic + 1) * 128]
                for n in range(2):
                    pe_l = MM(pool.unit(units[n]), lh,
                              hdnc(ic)[:, n * 512:(n + 1) * 512],
                              start=(ic == 0), stop=(ic == NIC - 1),
                              inc="PE" if (ic == NIC - 1 and n == 1) else None)
            dn_pe_done[oc] = pe_l
            sch.wait("dv", "PE", pe_l)
            dv_ev = DV("scalar_tensor_tensor", sumT[:, oc * TQ:(oc + 1) * TQ],
                       P[units[0] // 2], bcol("ds", oc),
                       sumT[:, oc * TQ:(oc + 1) * TQ], ALU.mult, ALU.add)
            pool.release(units, "DV", dv_ev)
        A.release()
        tap_dump("sumT2", sumT)
        ckstop("E")

        sch.barrier()
        A.mark()
        FR = 4
        x_rt = [A.f32(D) for _ in range(FR)]
        stage = [A.f32(D) for _ in range(FR)]
        f_dv_done = {}
        f_st = {}
        for tb in range(8):
            r = tb % FR
            if tb >= FR:
                sch.wait("gp", "DV", f_dv_done[tb - FR])
            ldx = DMA("gp", x_rt[r], x_d[tb * 128:(tb + 1) * 128, :], "WT")
            units = (0, 1, 2, 3) if tb % 2 == 0 else (4, 5, 6, 7)
            pool.acquire(units)
            for oc in range(NKC):
                u = units[oc // 4]
                dst = pool.unit(u)[:, (oc % 4) * 128:(oc % 4) * 128 + 128]
                pe_l = MM(dst,
                          sumT[:, oc * TQ + tb * 128: oc * TQ + tb * 128 + 128],
                          ident, start=(oc % 4 == 0), stop=(oc % 4 == 3),
                          inc="PE" if oc == NKC - 1 else None)
            sch.wait("dv", "PE", pe_l)
            sch.wait("dv", "WT", ldx)
            if tb >= FR:
                sch.wait("dv", "LD", f_st[tb - FR])
            for half in range(2):
                dv_f = DV("tensor_tensor",
                          stage[r].rearrange("p (h c) -> p h c", h=2)[:, half, :],
                          P[units[2 * half] // 2],
                          x_rt[r].rearrange("p (h c) -> p h c", h=2)[:, half, :],
                          ALU.add)
            f_dv_done[tb] = dv_f
            pool.release(units, "DV", dv_f)
            sch.wait("sy", "DV", dv_f)
            f_st[tb] = DMA("sy", out_d[tb * 128:(tb + 1) * 128, :], stage[r],
                           "LD")
        A.release()
        sch.wait("sy", "LD")
        sch.wait("sy", "WT")

        if _trunc:
            sch.ops = sch.ops[:_trunc["n"]]
            if _trunc["ld"]:
                sch.ops.append(("sy", "wait", "LD", _trunc["ld"]))
            if _trunc["wt"]:
                sch.ops.append(("sy", "wait", "WT", _trunc["wt"]))
        block = e(nc.Block())

        def run_for(key):
            pad_nop = key == "dv"

            def prog(eng):
                prev_was_op = False
                for op in sch.ops:
                    if op[0] != key:
                        continue
                    if op[1] == "wait":
                        eng.wait_ge(sems[op[2]], op[3])
                        prev_was_op = False
                    else:
                        if pad_nop and prev_was_op:
                            eng.engine_nop()
                        inst = op[2](eng)
                        if op[3] is not None:
                            inst.then_inc(sems[op[3][0]], op[3][1])
                        prev_was_op = True
            return prog
        block.tensor(run_for("pe"))
        block.vector(run_for("dv"))
        block.scalar(run_for("ac"))
        block.sync(run_for("sy"))
        block.gpsimd(run_for("gp"))

    return nc



def _host_prep(inputs):
    x_full = np.asarray(inputs["L0_attn_h"], np.float32)
    shared = {
        "wdkv": np.ascontiguousarray(np.asarray(inputs["L0_attn_W_DKV"], np.float32)),
        "wdq": np.ascontiguousarray(np.asarray(inputs["L0_attn_W_DQ"], np.float32)),
        "wuk": np.ascontiguousarray(np.asarray(inputs["L0_attn_W_UK"], np.float32)),
        "wuv": np.ascontiguousarray(np.asarray(inputs["L0_attn_W_UV"], np.float32)),
        "wuq": np.ascontiguousarray(np.asarray(inputs["L0_attn_W_UQ"], np.float32)),
        "wkr": np.ascontiguousarray(np.asarray(inputs["L0_attn_W_KR"], np.float32)),
        "wo": np.ascontiguousarray(np.asarray(inputs["L0_attn_W_O"], np.float32)),
        "wg": np.ascontiguousarray(np.asarray(inputs["L0_ffn_gate_w"], np.float32).T),
        "wu": np.ascontiguousarray(np.asarray(inputs["L0_ffn_up_w"], np.float32).T),
        "wd": np.ascontiguousarray(np.asarray(inputs["L0_ffn_down_w"], np.float32).T),
        "ident": np.eye(128, dtype=np.float32).astype(BF),
        "buvrow": np.asarray(inputs["L0_attn_b_UV"], np.float32).astype(BF).reshape(1, D),
    }
    biasT = np.zeros((128, NBIAS), np.float32)
    def put(name, vec):
        vec = np.asarray(vec, np.float32)
        n = vec.size // 128
        biasT[:, _BCOLS[name]:_BCOLS[name] + n] = vec.reshape(n, 128).T
    put("bdkv", inputs["L0_attn_b_DKV"])
    put("bdq", inputs["L0_attn_b_DQ"])
    put("buk", inputs["L0_attn_b_UK"])
    put("buq", inputs["L0_attn_b_UQ"])
    put("bkr", inputs["L0_attn_b_KR"])
    put("bo", inputs["L0_attn_b_O"])
    put("gs", inputs["L0_ffn_gate_s"])
    put("us", inputs["L0_ffn_up_s"])
    put("ds", inputs["L0_ffn_down_s"])
    biasT[:, _BCOLS["negb"]] = NEGB
    biasT[:, _BCOLS["eps"]] = 1e-5
    shared["biasT"] = biasT
    rr = np.arange(128)
    shared["tri"] = np.where(rr[:, None] <= rr[None, :], 0.0,
                             -29952.0).astype(np.float32).astype(BF)

    in_maps = []
    for c in range(8):
        b, p = c // 2, c % 2
        ob = [2 * j + p for j in range(8)] + [2 * j + (1 - p) for j in range(8)]
        perm = np.concatenate([np.arange(o * 128, (o + 1) * 128) for o in ob])
        m = dict(shared)
        m["x"] = np.ascontiguousarray(x_full[b][perm])
        m["pmask"] = np.full((128, 128), 0.0 if p == 1 else -29952.0,
                             np.float32).astype(BF)
        pos = perm.astype(np.float64)
        inv = 1.0 / (10000.0 ** (np.arange(64, dtype=np.float64) * 2.0 / 128.0))
        ang = inv[:, None] * pos[None, :]
        c2 = np.repeat(np.cos(ang), 2, axis=0)
        s2 = np.repeat(np.sin(ang), 2, axis=0)
        s2[0::2] *= -1.0
        m["c2k"] = c2.astype(np.float32).astype(BF)
        m["s2k"] = s2.astype(np.float32).astype(BF)
        in_maps.append(m)
    return in_maps


def kernel(**inputs):
    in_maps = _host_prep(inputs)
    nc = build_program()
    res = run_bass_kernel_spmd(nc, in_maps, core_ids=list(range(8)))
    out = np.empty((4, S, D), np.float32)
    for c in range(8):
        b, p = c // 2, c % 2
        r = np.asarray(res.results[c]["out"], np.float32)
        for j in range(8):
            o = 2 * j + p
            out[b, o * 128:(o + 1) * 128, :] = r[j * 128:(j + 1) * 128, :]
    return out


# version 14
# speedup vs baseline: 1.1922x; 1.1641x over previous
import numpy as np
import ml_dtypes
from collections import defaultdict
from contextlib import ExitStack

import concourse.bass as bass
import concourse.mybir as mybir
from concourse.bass_utils import run_bass_kernel_spmd

F32 = mybir.dt.float32
BF16 = mybir.dt.bfloat16
AF = mybir.ActivationFunctionType
ALU = mybir.AluOpType
AXX = mybir.AxisListType.X

BF = ml_dtypes.bfloat16

S = 2048
D = 2048
H = 16
HD = 128
DKV = 512
DQ = 1536
INTER = 5632
TQ = 1024
NKC = D // 128
NIC = INTER // 128
NEGB = -15.0
SCALE = 1.0 / 16.0
SHUF_SWAP = [i ^ 1 for i in range(32)]

_BCOLS = {}
_nb = 0
def _bc(name, n):
    global _nb
    _BCOLS[name] = _nb
    _nb += n
_bc("bdkv", 4); _bc("bdq", 12); _bc("buk", 16); _bc("buq", 16); _bc("bkr", 16)
_bc("bo", 16); _bc("gs", 44); _bc("us", 44); _bc("ds", 16); _bc("negb", 1)
_bc("eps", 1)
NBIAS = _nb


class Sched:
    ENGS = ("pe", "dv", "ac", "sy", "gp")

    def __init__(self):
        self.ops = []
        self.cnt = defaultdict(int)

    def do(self, eng, fn, inc=None, n=1):
        assert eng in self.ENGS
        self.ops.append((eng, "op", fn, (inc, n) if inc else None))
        if inc:
            self.cnt[inc] += n
            return self.cnt[inc]
        return None

    def wait(self, eng, sem, val=None):
        if val is None:
            val = self.cnt[sem]
        if val is None or val <= 0:
            return
        self.ops.append((eng, "wait", sem, val))

    def now(self, sem):
        return self.cnt[sem]

    def barrier(self):
        snap = {s: self.cnt[s] for s in ("PE", "DV", "AC", "LD", "WT")}
        for e in self.ENGS:
            for s_, v in snap.items():
                self.wait(e, s_, v)


class Arena:

    def __init__(self, handle, total):
        self.ap_full = handle.ap()
        self.total = total
        self.off = 0
        self.marks = []

    def bf(self, n):
        n = (n + 63) & ~63
        assert self.off + n <= self.total, f"arena overflow {self.off}+{n}"
        ap = self.ap_full[:, self.off:self.off + n]
        self.off += n
        return ap

    def f32(self, n):
        n2 = 2 * ((n + 31) & ~31)
        assert self.off + n2 <= self.total, f"arena overflow {self.off}+{n2}"
        ap = self.ap_full[:, self.off:self.off + n2].bitcast(F32)
        self.off += n2
        return ap

    def at_bf(self, off, n):
        assert off + n <= self.total
        return self.ap_full[:, off:off + n]

    def mark(self):
        self.marks.append(self.off)

    def release(self):
        self.off = self.marks.pop()


class PsumPool:

    def __init__(self, sched, tensors):
        self.s = sched
        self.P = tensors
        self.free_at = [None] * 8

    def unit(self, u):
        return self.P[u // 2][:, (u % 2) * 512:(u % 2) * 512 + 512]

    def acquire(self, units):
        for u in units:
            fa = self.free_at[u]
            if fa:
                self.s.wait("pe", fa[0], fa[1])
                self.free_at[u] = None

    def release(self, units, sem, cnt):
        for u in units:
            self.free_at[u] = (sem, cnt)


def seg512(a, b):
    out = []
    while a < b:
        e = min((a // 512 + 1) * 512, b)
        out.append((a, e))
        a = e
    return out


class _Stop(Exception):
    pass


def build_program(taps=(), stop_after=None):
    nc = bass.Bass("TRN2", target_bir_lowering=False)
    taps = set(taps)

    x_d = nc.dram_tensor("x", [S, D], F32, kind="ExternalInput").ap()
    wdkv_d = nc.dram_tensor("wdkv", [D, DKV], F32, kind="ExternalInput").ap()
    wdq_d = nc.dram_tensor("wdq", [D, DQ], F32, kind="ExternalInput").ap()
    wuk_d = nc.dram_tensor("wuk", [DKV, D], F32, kind="ExternalInput").ap()
    wuv_d = nc.dram_tensor("wuv", [DKV, D], F32, kind="ExternalInput").ap()
    wuq_d = nc.dram_tensor("wuq", [DQ, D], F32, kind="ExternalInput").ap()
    wkr_d = nc.dram_tensor("wkr", [D, D], F32, kind="ExternalInput").ap()
    wo_d = nc.dram_tensor("wo", [D, D], F32, kind="ExternalInput").ap()
    wg_d = nc.dram_tensor("wg", [D, INTER], F32, kind="ExternalInput").ap()
    wu_d = nc.dram_tensor("wu", [D, INTER], F32, kind="ExternalInput").ap()
    wd_d = nc.dram_tensor("wd", [INTER, D], F32, kind="ExternalInput").ap()
    biasT_d = nc.dram_tensor("biasT", [128, NBIAS], F32, kind="ExternalInput").ap()
    ident_d = nc.dram_tensor("ident", [128, 128], BF16, kind="ExternalInput").ap()
    tri_d = nc.dram_tensor("tri", [128, 128], BF16, kind="ExternalInput").ap()
    pmask_d = nc.dram_tensor("pmask", [128, 128], BF16, kind="ExternalInput").ap()
    c2k_d = nc.dram_tensor("c2k", [128, S], BF16, kind="ExternalInput").ap()
    s2k_d = nc.dram_tensor("s2k", [128, S], BF16, kind="ExternalInput").ap()
    buv_d = nc.dram_tensor("buvrow", [1, D], BF16, kind="ExternalInput").ap()
    ctx_hbm = nc.dram_tensor(
        "ctxscr", [H * HD, TQ], BF16,
        kind=("ExternalOutput" if "ctx" in taps else "Internal")).ap()
    out_d = nc.dram_tensor("out", [TQ, D], F32, kind="ExternalOutput").ap()
    lscr_d = nc.dram_tensor("lscr", [1, TQ], F32, kind="Internal").ap()

    tap_d = {}
    def decl_tap(name, shape, dt):
        if name in taps:
            tap_d[name] = nc.dram_tensor("tap_" + name, shape, dt,
                                         kind="ExternalOutput").ap()
    decl_tap("xnT", [128, NKC * S], BF16)
    decl_tap("ckvT", [128, 4 * S], BF16)
    decl_tap("cqT", [128, 12 * TQ], BF16)
    decl_tap("kCT", [128, S], BF16)
    decl_tap("kRT", [128, S], BF16)
    decl_tap("vG", [128, 16 * 512], BF16)
    decl_tap("qCT", [128, TQ], BF16)
    decl_tap("qRT", [128, TQ], BF16)
    decl_tap("sumT", [128, NKC * TQ], BF16)
    decl_tap("sumT2", [128, NKC * TQ], BF16)

    sch = Sched()

    with ExitStack() as es:
        e = es.enter_context
        ARENA_N = 106368
        arena_h = e(nc.sbuf_tensor("arena", [128, ARENA_N], BF16))
        A = Arena(arena_h, ARENA_N)
        P = [e(nc.psum_tensor(f"psum{i}", [128, 1024], F32)).ap()
             for i in range(4)]
        pool = PsumPool(sch, P)
        sems = {s_: e(nc.semaphore("sem" + s_))
                for s_ in ("LD", "WT", "PE", "DV", "AC")}

        biases = A.f32(NBIAS)
        ident = A.bf(128)
        tri = A.bf(128)
        pmask = A.bf(128)
        ones_bf = A.bf(64)[:, 0:1]
        ones1_bf = A.bf(128)[0:1, :]
        ones1_f = A.f32(128)[0:1, :]
        xnT_off = A.off
        xnT = A.bf(NKC * S)

        def xnt(kc):
            return xnT[:, kc * S:(kc + 1) * S]

        def bcol(name, i=0):
            c = _BCOLS[name] + i
            return biases[:, c:c + 1]

        def MM(out, lh, rh, start, stop, inc=None):
            return sch.do("pe", lambda t, o=out, l=lh, r=rh, s1=start, s2=stop:
                          t.matmul(o, l, r, start=s1, stop=s2), inc)

        def ACT(out, in_, func, bias=0.0, scale=1.0, accum=None, inc="AC"):
            return sch.do("ac", lambda a, o=out, i=in_, f=func, b=bias, s=scale,
                          ac_=accum: a.activation(out=o, in_=i, func=f, bias=b,
                                                  scale=s, accum_out=ac_), inc)

        def DV(meth, *args, inc="DV", **kw):
            def fn(v, m=meth, a=args, k=dict(kw)):
                return getattr(v, m)(*a, **k)
            return sch.do("dv", fn, inc)

        def DMA(q, out, in_, sem):
            return sch.do(q, lambda g, o=out, i=in_: g.dma_start(out=o, in_=i),
                          sem, 16)

        def tap_dump(name, src_ap):
            if name in tap_d:
                sch.barrier()
                DMA("sy", tap_d[name], src_ap, "LD")

        def wslice(dram, col0, ncols, nch, p=128):
            return dram[:, col0:col0 + ncols].rearrange(
                "(kc p) c -> p kc c", p=p)

        def r3(sbuf_ap, ncols):
            return sbuf_ap.rearrange("p (kc c) -> p kc c", c=ncols)

        _trunc = {}

        def ckstop(ph):
            if stop_after == ph and not _trunc:
                _trunc["n"] = len(sch.ops)
                _trunc["ld"] = sch.cnt["LD"]
                _trunc["wt"] = sch.cnt["WT"]

        DMA("sy", biases[:, 0:NBIAS], biasT_d, "LD")
        DMA("sy", ident, ident_d, "LD")
        DMA("sy", tri, tri_d, "LD")
        DMA("sy", pmask, pmask_d, "LD")
        setup_ld = sch.now("LD")
        DV("memset", ones_bf, 1.0)
        DV("memset", ones1_bf, 1.0)
        DV("memset", ones1_f, 1.0)

        A.mark()
        abc_base = A.off
        c2k = A.bf(S)
        s2k = A.bf(S)
        buv_row = A.bf(D)[0:1, :]
        ckvT = A.bf(4 * S)
        cqT = A.bf(12 * TQ)
        DMA("sy", c2k, c2k_d, "LD")
        DMA("sy", s2k, s2k_d, "LD")
        DMA("sy", buv_row, buv_d, "LD")
        tables_ld = sch.now("LD")

        ctxT = A.at_bf(abc_base, NKC * TQ)
        wo_r = [A.at_bf(abc_base + NKC * TQ + i * 2048, 2048)
                for i in range(2)]
        sumT = A.at_bf(abc_base + NKC * TQ + 4096, NKC * TQ)
        hdnB = abc_base + NKC * TQ + 4096 + NKC * TQ

        def hdnc(ic):
            if ic < 16:
                return ctxT[:, ic * TQ:(ic + 1) * TQ]
            if ic < 42:
                return A.at_bf(hdnB + (ic - 16) * TQ, TQ)
            return A.at_bf(abc_base + NKC * TQ + (ic - 42) * TQ, TQ)

        silu_scr = A.at_bf(abc_base + NKC * TQ + 2048, 2048).bitcast(F32)
        wg_r = [A.at_bf(ARENA_N - 2048, 2048), A.at_bf(ARENA_N - 4096, 2048)]
        wu_r = [A.at_bf(ARENA_N - 6144, 2048), A.at_bf(ARENA_N - 8192, 2048)]
        assert hdnB + 26 * TQ <= ARENA_N - 8192

        A.mark()
        AR = 4
        x_t = [A.f32(D) for _ in range(AR)]
        xn_tm = [A.bf(D) for _ in range(AR)]
        sfull = A.f32(32 * AR)
        st6 = [sfull[:, 32 * i:32 * i + 24] for i in range(AR)]
        st2c = [sfull[:, 32 * i + 24:32 * i + 26] for i in range(AR)]
        wdkv_all = A.bf(4 * NKC * 128)
        wdq_r = [A.bf(NKC * 128) for _ in range(4)]

        a_dv_done = {}
        a_ac_done = {}
        a_pe_done = {}
        sch.wait("dv", "LD", setup_ld)
        sch.wait("pe", "LD", setup_ld)
        sch.wait("ac", "LD", setup_ld)
        for oc in range(4):
            wt = DMA("gp", r3(wdkv_all[:, oc * NKC * 128:(oc + 1) * NKC * 128],
                              128),
                     wslice(wdkv_d, oc * 128, 128, NKC), "WT")
        wdkv_wt = sch.now("WT")
        wdq_wt = {}
        wdq_pe = {}

        def emit_tile_head(t):
            r = t % AR
            if t >= AR:
                sch.wait("sy", "DV", a_dv_done[t - AR])
            ld = DMA("sy", x_t[r], x_d[t * 128:(t + 1) * 128, :], "LD")
            sch.wait("dv", "LD", ld)
            for j in range(4):
                DV("bn_stats", st6[r][:, 6 * j:6 * j + 6],
                   x_t[r][:, 512 * j:512 * j + 512])
            dv_agg = DV("bn_aggr", st2c[r], st6[r])
            sch.wait("ac", "DV", dv_agg)
            ac_ln = ACT(st2c[r][:, 1:2], st2c[r][:, 1:2], AF.Ln,
                        bias=bcol("eps"))
            sch.wait("ac", "AC", ac_ln)
            ac_rstd = ACT(st2c[r][:, 1:2], st2c[r][:, 1:2], AF.Exp,
                          scale=-0.5)
            sch.wait("dv", "AC", ac_rstd)
            if t >= AR:
                sch.wait("dv", "PE", a_pe_done[t - AR])
            dv_app = DV("tensor_scalar", xn_tm[r], x_t[r], st2c[r][:, 0:1],
                        st2c[r][:, 1:2], ALU.subtract, ALU.mult)
            a_dv_done[t] = dv_app

        def emit_tile_tail(t):
            r = t % AR
            sch.wait("pe", "DV", a_dv_done[t])
            for half in range(2):
                units = (0, 1) if half == 0 else (2, 3)
                pool.acquire(units)
                for j in range(8):
                    kc = 8 * half + j
                    u = units[j // 4]
                    dst = pool.unit(u)[:, (j % 4) * 128:(j % 4) * 128 + 128]
                    pe_last = MM(dst, xn_tm[r][:, kc * 128:(kc + 1) * 128],
                                 ident, start=(j % 4 == 0), stop=(j % 4 == 3),
                                 inc="PE" if j == 7 else None)
                if half == 1:
                    a_pe_done[t] = pe_last
                sch.wait("ac", "PE", pe_last)
                pv = P[0] if half == 0 else P[1]
                src_ = pv.rearrange("p (kc c) -> p kc c", c=128)
                dst_ = xnT.rearrange("p (kc tt) -> p kc tt", tt=S)[
                    :, 8 * half:8 * half + 8, t * 128:(t + 1) * 128]
                ac_ev = ACT(dst_, src_, AF.Identity)
                pool.release(units, "AC", ac_ev)
            a_ac_done[t] = ac_ev

        def emit_wdq_dma(n, oc):
            prev = wdq_pe.get((n, oc - 4)) if oc >= 4 else (
                wdq_pe.get((0, oc + 8)) if n == 1 else None)
            if prev:
                sch.wait("gp", "PE", prev)
            wdq_wt[(n, oc)] = DMA("gp", r3(wdq_r[oc % 4], 128),
                                  wslice(wdq_d, oc * 128, 128, NKC), "WT")

        def emit_bsec(n):
            sch.wait("pe", "AC", a_ac_done[4 * n + 3])
            if n == 0:
                sch.wait("pe", "WT", wdkv_wt)
            chunks = [("kv", oc) for oc in range(4)]
            if n < 2:
                chunks += [("q", oc) for oc in range(12)]
                for oc in range(4):
                    emit_wdq_dma(n, oc)
            for k, (kind, oc) in enumerate(chunks):
                u = 4 + k % 4
                pool.acquire((u,))
                if kind == "q":
                    sch.wait("pe", "WT", wdq_wt[(n, oc)])
                for kc in range(NKC):
                    lh = (wdkv_all[:, (oc * NKC + kc) * 128:
                                   (oc * NKC + kc) * 128 + 128]
                          if kind == "kv" else
                          wdq_r[oc % 4][:, kc * 128:(kc + 1) * 128])
                    pe_l = MM(pool.unit(u), lh,
                              xnt(kc)[:, n * 512:(n + 1) * 512],
                              start=(kc == 0), stop=(kc == NKC - 1),
                              inc="PE" if kc == NKC - 1 else None)
                if kind == "q":
                    wdq_pe[(n, oc)] = pe_l
                    if oc + 4 < 12:
                        emit_wdq_dma(n, oc + 4)
                sch.wait("ac", "PE", pe_l)
                if kind == "kv":
                    dst = ckvT[:, oc * S + n * 512: oc * S + (n + 1) * 512]
                    b = bcol("bdkv", oc)
                else:
                    dst = cqT[:, oc * TQ + n * 512: oc * TQ + (n + 1) * 512]
                    b = bcol("bdq", oc)
                ac_ev = ACT(dst, pool.unit(u), AF.Identity, bias=b)
                pool.release((u,), "AC", ac_ev)

        for t in range(4):
            emit_tile_head(t)
            emit_tile_tail(t)
        for n in range(4):
            for t in range(4 * n + 4, min(4 * n + 8, 16)):
                emit_tile_head(t)
            emit_bsec(n)
            for t in range(4 * n + 4, min(4 * n + 8, 16)):
                emit_tile_tail(t)
        A.release()
        tap_dump("xnT", xnT)
        tap_dump("ckvT", ckvT)
        tap_dump("cqT", cqT)
        ckstop("B")

        sch.barrier()
        A.mark()
        kRT = [A.bf(S), A.bf(S)]
        kCT = [A.bf(S), A.bf(S)]
        vG = A.bf(16 * 512)
        qCT = [A.bf(TQ), A.bf(TQ)]
        qRT = [A.bf(TQ), A.bf(TQ)]
        swp = A.bf(S)
        Et = [A.bf(TQ), A.bf(TQ)]
        linv_rep = A.f32(TQ)
        ctx_h = A.bf(TQ)
        ctx_n = A.bf(TQ)
        wkr_r = [A.bf(NKC * 128), A.bf(NKC * 128)]
        wuk_r = [A.bf(4 * 128), A.bf(4 * 128)]
        wuv_r = A.bf(4 * 512)
        wuq_r = [A.bf(12 * 128), A.bf(12 * 128)]

        sch.wait("dv", "LD", tables_ld)
        sch.wait("pe", "LD", tables_ld)

        v_pe_done = {}
        att_end_pe = {}
        kr_pe_d = {}
        kc_pe_d = {}
        q_pe_d = {}
        dv_qrope_d = {}
        ac_kc_d = {}
        dv_ctx_d = {}
        ac_cu_d = {}
        ac_ex_d = {}
        bc_pe_d = {}
        ctx_store = {}
        wo_wt = {}

        def emit_vgroup(g):
            if g > 0:
                sch.wait("gp", "PE", v_pe_done[g - 1])
            wt = DMA("gp", r3(wuv_r, 512), wslice(wuv_d, g * 512, 512, 4),
                     "WT")
            sch.wait("pe", "WT", wt)
            if g > 0:
                sch.wait("ac", "PE", att_end_pe[4 * g - 1])
            uperm = (0, 1, 2, 3, 6, 7, 4, 5)
            for kb in range(16):
                u = uperm[kb % 8]
                pool.acquire((u,))
                for kc in range(4):
                    MM(pool.unit(u),
                       ckvT[:, kc * S + kb * 128: kc * S + kb * 128 + 128],
                       wuv_r[:, kc * 512:(kc + 1) * 512],
                       start=(kc == 0), stop=False)
                pe_l = MM(pool.unit(u), ones1_bf,
                          buv_row[:, g * 512:(g + 1) * 512],
                          start=False, stop=True, inc="PE")
                sch.wait("ac", "PE", pe_l)
                ac_ev = ACT(vG[:, kb * 512:(kb + 1) * 512], pool.unit(u),
                            AF.Identity)
                pool.release((u,), "AC", ac_ev)
            v_pe_done[g] = sch.now("PE")
            if g == 0:
                tap_dump("vG", vG)

        def emit_prod(hh):
            r = hh % 2
            if hh >= 2:
                sch.wait("ac", "PE", att_end_pe[hh - 2])
                sch.wait("dv", "PE", att_end_pe[hh - 2])
                sch.wait("gp", "PE", kr_pe_d[hh - 2])
            wt = DMA("gp", r3(wkr_r[r], 128),
                     wslice(wkr_d, hh * 128, 128, NKC), "WT")
            units = (0, 1, 2, 3)
            pool.acquire(units)
            sch.wait("pe", "WT", wt)
            for kc in range(NKC):
                lh = wkr_r[r][:, kc * 128:(kc + 1) * 128]
                for n in range(4):
                    pe_l = MM(pool.unit(units[n]),
                              lh, xnt(kc)[:, n * 512:(n + 1) * 512],
                              start=(kc == 0), stop=(kc == NKC - 1),
                              inc="PE" if (kc == NKC - 1 and n == 3) else None)
            kr_pe_d[hh] = pe_l
            sch.wait("ac", "PE", pe_l)
            for half in range(2):
                ac_ev = ACT(kRT[r][:, half * 1024:half * 1024 + 1024],
                            P[units[2 * half] // 2], AF.Identity,
                            bias=bcol("bkr", hh))
            pool.release(units, "AC", ac_ev)
            sch.wait("dv", "AC", ac_ev)
            DV("stream_shuffle", swp, kRT[r], SHUF_SWAP)
            DV("tensor_tensor", kRT[r], kRT[r], c2k, ALU.mult)
            DV("tensor_tensor", swp, swp, s2k, ALU.mult)
            DV("tensor_tensor", kRT[r], kRT[r], swp, ALU.add)
            if hh >= 2:
                sch.wait("gp", "PE", kc_pe_d[hh - 2])
            wt = DMA("gp", r3(wuk_r[r], 128),
                     wslice(wuk_d, hh * 128, 128, 4), "WT")
            units = (4, 5, 6, 7)
            pool.acquire(units)
            sch.wait("pe", "WT", wt)
            for kc in range(4):
                lh = wuk_r[r][:, kc * 128:(kc + 1) * 128]
                for n in range(4):
                    pe_l = MM(pool.unit(units[n]), lh,
                              ckvT[:, kc * S + n * 512: kc * S + n * 512 + 512],
                              start=(kc == 0), stop=(kc == 3),
                              inc="PE" if (kc == 3 and n == 3) else None)
            kc_pe_d[hh] = pe_l
            sch.wait("ac", "PE", pe_l)
            for half in range(2):
                ac_kc = ACT(kCT[r][:, half * 1024:half * 1024 + 1024],
                            P[units[2 * half] // 2], AF.Identity,
                            bias=bcol("buk", hh))
            pool.release(units, "AC", ac_kc)
            ac_kc_d[hh] = ac_kc
            if hh >= 2:
                sch.wait("gp", "PE", q_pe_d[hh - 2])
            wt = DMA("gp", r3(wuq_r[r], 128),
                     wslice(wuq_d, hh * 128, 128, 12), "WT")
            units = (0, 1)
            pool.acquire(units)
            sch.wait("pe", "WT", wt)
            for kc in range(12):
                lh = wuq_r[r][:, kc * 128:(kc + 1) * 128]
                for n in range(2):
                    pe_l = MM(pool.unit(units[n]), lh,
                              cqT[:, kc * TQ + n * 512: kc * TQ + n * 512 + 512],
                              start=(kc == 0), stop=(kc == 11),
                              inc="PE" if (kc == 11 and n == 1) else None)
            q_pe_d[hh] = pe_l
            sch.wait("ac", "PE", pe_l)
            ac_q = ACT(qCT[r], P[0], AF.Identity, bias=bcol("buq", hh))
            pool.release(units, "AC", ac_q)
            sch.wait("dv", "AC", ac_q)
            DV("stream_shuffle", swp[:, 0:TQ], qCT[r], SHUF_SWAP)
            DV("tensor_tensor", qRT[r], qCT[r], c2k[:, 0:TQ], ALU.mult)
            DV("tensor_tensor", swp[:, 0:TQ], swp[:, 0:TQ], s2k[:, 0:TQ],
               ALU.mult)
            dv_qrope_d[hh] = DV("tensor_tensor", qRT[r], qRT[r],
                                swp[:, 0:TQ], ALU.add)
            if hh == 0:
                tap_dump("kRT", kRT[r])
                tap_dump("kCT", kCT[r])
                tap_dump("qCT", qCT[r])
                tap_dump("qRT", qRT[r])

        def emit_tail_b(hh):
            pool.acquire((4, 5))
            sch.wait("pe", "AC", ac_ex_d[hh])
            MM(P[2][:, 0:512], ones1_f, linv_rep[0:1, 0:512],
               start=True, stop=True)
            bc_pe = MM(P[2][:, 512:1024], ones1_f, linv_rep[0:1, 512:1024],
                       start=True, stop=True, inc="PE")
            bc_pe_d[hh] = bc_pe
            sch.wait("dv", "PE", bc_pe)
            sch.wait("dv", "AC", ac_cu_d[hh])
            if hh < 15:
                if hh >= 1:
                    sch.wait("dv", "LD", ctx_store[hh - 1])
                dv_ctx = DV("tensor_tensor", ctx_n, ctx_h, P[2], ALU.mult)
                dv_ctx_d[hh] = dv_ctx
                sch.wait("sy", "DV", dv_ctx)
                ctx_store[hh] = DMA("sy", ctx_hbm[hh * 128:(hh + 1) * 128, :],
                                    ctx_n, "LD")
            else:
                sch.wait("dv", "PE", q_pe_d[15])
                dv_ctx = DV("tensor_tensor", ctxT[:, 15 * TQ:16 * TQ],
                            ctx_h, P[2], ALU.mult)
                dv_ctx_d[hh] = dv_ctx
            pool.release((4, 5), "DV", dv_ctx)

        def emit_att(hh):
            oc = hh % 4
            r = hh % 2
            if hh >= 1:
                emit_tail_b(hh - 1)
            if hh == 15:
                sch.wait("sy", "PE", q_pe_d[15])
                sch.wait("sy", "DV", dv_qrope_d[15])
                sch.wait("gp", "PE", q_pe_d[15])
                sch.wait("gp", "DV", dv_qrope_d[15])
                for i in range(2):
                    wo_wt[i] = DMA("gp", r3(wo_r[i], 128),
                                   wslice(wo_d, i * 128, 128, NKC), "WT")
                for c in range(15):
                    q = "sy" if c % 2 == 0 else "gp"
                    sch.wait(q, "LD", ctx_store[c])
                    DMA(q, ctxT[:, c * TQ:(c + 1) * TQ],
                        ctx_hbm[c * 128:(c + 1) * 128, :],
                        "LD" if q == "sy" else "WT")
            pool.acquire((0, 1, 2, 3))
            sch.wait("pe", "DV", dv_qrope_d[hh])
            sch.wait("pe", "AC", ac_kc_d[hh])
            ctxl_acquired = [False]
            exp_done = {}
            ctxl_done = {}
            last_kb_bank = {0: 11, 1: 15}

            def emit_ctxl(kb):
                j0 = kb if kb < 8 else kb - 8
                q0 = j0 * 128
                par = kb % 2
                segs = seg512(q0, TQ)
                if not ctxl_acquired[0]:
                    pool.acquire((4, 5, 6, 7))
                    ctxl_acquired[0] = True
                sch.wait("pe", "AC", exp_done[kb])
                for a_, b_ in segs:
                    bank = a_ // 512
                    MM(P[2][:, a_:b_],
                       vG[:, kb * 512 + oc * 128: kb * 512 + oc * 128 + 128],
                       Et[par][:, a_:b_], start=(kb == 0),
                       stop=(kb == last_kb_bank[bank]))
                for i_, (a_, b_) in enumerate(segs):
                    bank = a_ // 512
                    pe_cl = MM(P[3][0:1, a_:b_], ones_bf, Et[par][:, a_:b_],
                               start=(kb == 0),
                               stop=(kb == last_kb_bank[bank]),
                               inc="PE" if i_ == len(segs) - 1 else None)
                ctxl_done[kb] = pe_cl

            for kb in range(16):
                j0 = kb if kb < 8 else kb - 8
                q0 = j0 * 128
                par = kb % 2
                Spar = P[par]
                if kb >= 2:
                    sch.wait("pe", "AC", exp_done[kb - 2])
                segs = seg512(q0, TQ)
                q0bank = q0 // 512
                mk = tri if kb < 8 else pmask
                MM(Spar[:, q0:q0 + 128], ident, mk, start=True, stop=False)
                for a_, b_ in segs:
                    MM(Spar[:, a_:b_],
                       kCT[r][:, kb * 128:kb * 128 + 128],
                       qCT[r][:, a_:b_],
                       start=(a_ // 512 != q0bank), stop=False)
                for i_, (a_, b_) in enumerate(segs):
                    pe_sc = MM(Spar[:, a_:b_],
                               kRT[r][:, kb * 128:kb * 128 + 128],
                               qRT[r][:, a_:b_], start=False, stop=True,
                               inc="PE" if i_ == len(segs) - 1 else None)
                sch.wait("ac", "PE", pe_sc)
                if kb >= 2:
                    sch.wait("ac", "PE", ctxl_done[kb - 2])
                ac_exp = ACT(Et[par][:, q0:TQ], Spar[:, q0:TQ], AF.Exp,
                             bias=bcol("negb"), scale=SCALE)
                exp_done[kb] = ac_exp
                if kb >= 1:
                    emit_ctxl(kb - 1)
            emit_ctxl(15)
            sch.wait("ac", "PE", ctxl_done[15])
            if hh >= 1:
                sch.wait("ac", "DV", dv_ctx_d[hh - 1])
            ac_cu = ACT(ctx_h, P[2], AF.Identity)
            ac_cu_d[hh] = ac_cu
            if hh >= 1:
                sch.wait("ac", "PE", bc_pe_d[hh - 1])
            ac_ln = ACT(linv_rep[0:1, :], P[3][0:1, :], AF.Ln)
            sch.wait("ac", "AC", ac_ln)
            ac_ex_d[hh] = ACT(linv_rep[0:1, :], linv_rep[0:1, :], AF.Exp,
                              scale=-1.0)
            att_end_pe[hh] = ctxl_done[15]
            pool.release((0, 1), "AC", exp_done[14])
            pool.release((2, 3), "AC", exp_done[15])
            pool.release((4, 5), "AC", ac_cu)
            pool.release((6, 7), "AC", ac_ex_d[hh])

        for g in range(4):
            emit_vgroup(g)
            for oc in range(4):
                hh = 4 * g + oc
                emit_prod(hh)
                if oc >= 1:
                    emit_att(hh - 1)
            emit_att(4 * g + 3)
        emit_tail_b(15)
        A.release()
        A.release()
        ckstop("C")

        rl_ld = sch.now("LD")
        rl_wt = sch.now("WT")
        wgwu_wt = {}
        for ic in range(2):
            wgwu_wt[("g", ic)] = DMA("gp", r3(wg_r[ic], 128),
                                     wslice(wg_d, ic * 128, 128, NKC), "WT")
            wgwu_wt[("u", ic)] = DMA("gp", r3(wu_r[ic], 128),
                                     wslice(wu_d, ic * 128, 128, NKC), "WT")
        sch.wait("pe", "LD", rl_ld)
        sch.wait("pe", "WT", rl_wt)
        sch.wait("pe", "DV", dv_ctx_d[15])
        d_pe_done = {}
        for oc in range(NKC):
            r = oc % 2
            if oc >= 2:
                sch.wait("gp", "PE", d_pe_done[oc - 2])
                wo_wt[oc] = DMA("gp", r3(wo_r[r], 128),
                                wslice(wo_d, oc * 128, 128, NKC), "WT")
            units = (0, 1) if r == 0 else (2, 3)
            pool.acquire(units)
            sch.wait("pe", "WT", wo_wt[oc])
            for kc in range(NKC):
                lh = wo_r[r][:, kc * 128:(kc + 1) * 128]
                for n in range(2):
                    pe_l = MM(pool.unit(units[n]), lh,
                              ctxT[:, kc * TQ + n * 512: kc * TQ + n * 512 + 512],
                              start=(kc == 0), stop=(kc == NKC - 1),
                              inc="PE" if (kc == NKC - 1 and n == 1) else None)
            d_pe_done[oc] = pe_l
            sch.wait("ac", "PE", pe_l)
            ac_ev = ACT(sumT[:, oc * TQ:(oc + 1) * TQ], P[units[0] // 2],
                        AF.Identity, bias=bcol("bo", oc))
            pool.release(units, "AC", ac_ev)
        tap_dump("sumT", sumT)
        ckstop("D")

        e_pe_done = {}
        e_dv_done = {}
        for ic in range(NIC):
            er = ic % 2
            if ic >= 2:
                sch.wait("gp", "PE", e_pe_done[("g", ic - 2)])
                wt = DMA("gp", r3(wg_r[er], 128),
                         wslice(wg_d, ic * 128, 128, NKC), "WT")
            else:
                wt = wgwu_wt[("g", ic)]
            units = (0, 1) if ic % 2 == 0 else (4, 5)
            uu = (2, 3) if ic % 2 == 0 else (6, 7)
            pool.acquire(units)
            sch.wait("pe", "WT", wt)
            for kc in range(NKC):
                lh = wg_r[er][:, kc * 128:(kc + 1) * 128]
                for n in range(2):
                    pe_g = MM(pool.unit(units[n]), lh,
                              xnt(kc)[:, n * 512:(n + 1) * 512],
                              start=(kc == 0), stop=(kc == NKC - 1),
                              inc="PE" if (kc == NKC - 1 and n == 1) else None)
            e_pe_done[("g", ic)] = pe_g
            if ic >= 2:
                sch.wait("gp", "PE", e_pe_done[("u", ic - 2)])
                wt = DMA("gp", r3(wu_r[er], 128),
                         wslice(wu_d, ic * 128, 128, NKC), "WT")
            else:
                wt = wgwu_wt[("u", ic)]
            pool.acquire(uu)
            sch.wait("pe", "WT", wt)
            for kc in range(NKC):
                lh = wu_r[er][:, kc * 128:(kc + 1) * 128]
                for n in range(2):
                    pe_u = MM(pool.unit(uu[n]), lh,
                              xnt(kc)[:, n * 512:(n + 1) * 512],
                              start=(kc == 0), stop=(kc == NKC - 1),
                              inc="PE" if (kc == NKC - 1 and n == 1) else None)
            e_pe_done[("u", ic)] = pe_u
            sch.wait("ac", "PE", pe_g)
            if ic >= 1:
                sch.wait("ac", "DV", e_dv_done[ic - 1])
            ac_si = ACT(silu_scr, P[units[0] // 2], AF.Silu,
                        scale=bcol("gs", ic))
            sch.wait("dv", "PE", pe_u)
            sch.wait("dv", "AC", ac_si)
            dv_h = DV("scalar_tensor_tensor", hdnc(ic), P[uu[0] // 2],
                      bcol("us", ic), silu_scr, ALU.mult, ALU.mult)
            e_dv_done[ic] = dv_h
            pool.release(units, "AC", ac_si)
            pool.release(uu, "DV", dv_h)
        wd_r = [A.at_bf(xnT_off, NIC * 128),
                A.at_bf(xnT_off + NIC * 128 + 64, NIC * 128)]
        xnT_dead = e_pe_done[("u", NIC - 1)]
        sch.wait("gp", "PE", xnT_dead)
        wd0_wt = DMA("gp", r3(wd_r[0], 128), wslice(wd_d, 0, 128, NIC), "WT")
        dn_pe_done = {}
        for oc in range(NKC):
            r = oc % 2
            if oc == 0:
                sch.wait("pe", "WT", wd0_wt)
                sch.wait("pe", "DV", e_dv_done[NIC - 1])
                wt = None
            else:
                if oc >= 2:
                    sch.wait("gp", "PE", dn_pe_done[oc - 2])
                wt = DMA("gp", r3(wd_r[r], 128),
                         wslice(wd_d, oc * 128, 128, NIC), "WT")
            units = (0, 1) if r == 0 else (2, 3)
            pool.acquire(units)
            if wt is not None:
                sch.wait("pe", "WT", wt)
            for ic in range(NIC):
                lh = wd_r[r][:, ic * 128:(ic + 1) * 128]
                for n in range(2):
                    pe_l = MM(pool.unit(units[n]), lh,
                              hdnc(ic)[:, n * 512:(n + 1) * 512],
                              start=(ic == 0), stop=(ic == NIC - 1),
                              inc="PE" if (ic == NIC - 1 and n == 1) else None)
            dn_pe_done[oc] = pe_l
            sch.wait("dv", "PE", pe_l)
            dv_ev = DV("scalar_tensor_tensor", sumT[:, oc * TQ:(oc + 1) * TQ],
                       P[units[0] // 2], bcol("ds", oc),
                       sumT[:, oc * TQ:(oc + 1) * TQ], ALU.mult, ALU.add)
            pool.release(units, "DV", dv_ev)
        tap_dump("sumT2", sumT)
        ckstop("E")

        sch.barrier()
        FR = 4
        x_rt = [A.at_bf(xnT_off + i * 4096, 4096).bitcast(F32)
                for i in range(FR)]
        stage = [A.at_bf(xnT_off + 16384 + i * 4096, 4096).bitcast(F32)
                 for i in range(FR)]
        f_dv_done = {}
        f_st = {}
        for tb in range(8):
            r = tb % FR
            if tb >= FR:
                sch.wait("gp", "DV", f_dv_done[tb - FR])
            ldx = DMA("gp", x_rt[r], x_d[tb * 128:(tb + 1) * 128, :], "WT")
            units = (0, 1, 2, 3) if tb % 2 == 0 else (4, 5, 6, 7)
            pool.acquire(units)
            for oc in range(NKC):
                u = units[oc // 4]
                dst = pool.unit(u)[:, (oc % 4) * 128:(oc % 4) * 128 + 128]
                pe_l = MM(dst,
                          sumT[:, oc * TQ + tb * 128: oc * TQ + tb * 128 + 128],
                          ident, start=(oc % 4 == 0), stop=(oc % 4 == 3),
                          inc="PE" if oc == NKC - 1 else None)
            sch.wait("dv", "PE", pe_l)
            sch.wait("dv", "WT", ldx)
            if tb >= FR:
                sch.wait("dv", "LD", f_st[tb - FR])
            for half in range(2):
                dv_f = DV("tensor_tensor",
                          stage[r].rearrange("p (h c) -> p h c", h=2)[:, half, :],
                          P[units[2 * half] // 2],
                          x_rt[r].rearrange("p (h c) -> p h c", h=2)[:, half, :],
                          ALU.add)
            f_dv_done[tb] = dv_f
            pool.release(units, "DV", dv_f)
            sch.wait("sy", "DV", dv_f)
            f_st[tb] = DMA("sy", out_d[tb * 128:(tb + 1) * 128, :], stage[r],
                           "LD")
        sch.wait("sy", "LD")
        sch.wait("sy", "WT")

        if _trunc:
            sch.ops = sch.ops[:_trunc["n"]]
            if _trunc["ld"]:
                sch.ops.append(("sy", "wait", "LD", _trunc["ld"]))
            if _trunc["wt"]:
                sch.ops.append(("sy", "wait", "WT", _trunc["wt"]))
        block = e(nc.Block())

        def run_for(key):
            pad_nop = key == "dv"

            def prog(eng):
                prev_was_op = False
                for op in sch.ops:
                    if op[0] != key:
                        continue
                    if op[1] == "wait":
                        eng.wait_ge(sems[op[2]], op[3])
                        prev_was_op = False
                    else:
                        if pad_nop and prev_was_op:
                            eng.engine_nop()
                        inst = op[2](eng)
                        if op[3] is not None:
                            inst.then_inc(sems[op[3][0]], op[3][1])
                        prev_was_op = True
            return prog
        block.tensor(run_for("pe"))
        block.vector(run_for("dv"))
        block.scalar(run_for("ac"))
        block.sync(run_for("sy"))
        block.gpsimd(run_for("gp"))

    return nc



def _host_prep(inputs):
    x_full = np.asarray(inputs["L0_attn_h"], np.float32)
    shared = {
        "wdkv": np.ascontiguousarray(np.asarray(inputs["L0_attn_W_DKV"], np.float32)),
        "wdq": np.ascontiguousarray(np.asarray(inputs["L0_attn_W_DQ"], np.float32)),
        "wuk": np.ascontiguousarray(np.asarray(inputs["L0_attn_W_UK"], np.float32)),
        "wuv": np.ascontiguousarray(np.asarray(inputs["L0_attn_W_UV"], np.float32)),
        "wuq": np.ascontiguousarray(np.asarray(inputs["L0_attn_W_UQ"], np.float32)),
        "wkr": np.ascontiguousarray(np.asarray(inputs["L0_attn_W_KR"], np.float32)),
        "wo": np.ascontiguousarray(np.asarray(inputs["L0_attn_W_O"], np.float32)),
        "wg": np.ascontiguousarray(np.asarray(inputs["L0_ffn_gate_w"], np.float32).T),
        "wu": np.ascontiguousarray(np.asarray(inputs["L0_ffn_up_w"], np.float32).T),
        "wd": np.ascontiguousarray(np.asarray(inputs["L0_ffn_down_w"], np.float32).T),
        "ident": np.eye(128, dtype=np.float32).astype(BF),
        "buvrow": np.asarray(inputs["L0_attn_b_UV"], np.float32).astype(BF).reshape(1, D),
    }
    biasT = np.zeros((128, NBIAS), np.float32)
    def put(name, vec):
        vec = np.asarray(vec, np.float32)
        n = vec.size // 128
        biasT[:, _BCOLS[name]:_BCOLS[name] + n] = vec.reshape(n, 128).T
    put("bdkv", inputs["L0_attn_b_DKV"])
    put("bdq", inputs["L0_attn_b_DQ"])
    put("buk", inputs["L0_attn_b_UK"])
    put("buq", inputs["L0_attn_b_UQ"])
    put("bkr", inputs["L0_attn_b_KR"])
    put("bo", inputs["L0_attn_b_O"])
    put("gs", inputs["L0_ffn_gate_s"])
    put("us", inputs["L0_ffn_up_s"])
    put("ds", inputs["L0_ffn_down_s"])
    biasT[:, _BCOLS["negb"]] = NEGB
    biasT[:, _BCOLS["eps"]] = 1e-5
    shared["biasT"] = biasT
    rr = np.arange(128)
    shared["tri"] = np.where(rr[:, None] <= rr[None, :], 0.0,
                             -29952.0).astype(np.float32).astype(BF)

    in_maps = []
    for c in range(8):
        b, p = c // 2, c % 2
        ob = [2 * j + p for j in range(8)] + [2 * j + (1 - p) for j in range(8)]
        perm = np.concatenate([np.arange(o * 128, (o + 1) * 128) for o in ob])
        m = dict(shared)
        m["x"] = np.ascontiguousarray(x_full[b][perm])
        m["pmask"] = np.full((128, 128), 0.0 if p == 1 else -29952.0,
                             np.float32).astype(BF)
        pos = perm.astype(np.float64)
        inv = 1.0 / (10000.0 ** (np.arange(64, dtype=np.float64) * 2.0 / 128.0))
        ang = inv[:, None] * pos[None, :]
        c2 = np.repeat(np.cos(ang), 2, axis=0)
        s2 = np.repeat(np.sin(ang), 2, axis=0)
        s2[0::2] *= -1.0
        m["c2k"] = c2.astype(np.float32).astype(BF)
        m["s2k"] = s2.astype(np.float32).astype(BF)
        in_maps.append(m)
    return in_maps


def kernel(**inputs):
    in_maps = _host_prep(inputs)
    nc = build_program()
    res = run_bass_kernel_spmd(nc, in_maps, core_ids=list(range(8)))
    out = np.empty((4, S, D), np.float32)
    for c in range(8):
        b, p = c // 2, c % 2
        r = np.asarray(res.results[c]["out"], np.float32)
        for j in range(8):
            o = 2 * j + p
            out[b, o * 128:(o + 1) * 128, :] = r[j * 128:(j + 1) * 128, :]
    return out


# version 15
# speedup vs baseline: 1.2140x; 1.0183x over previous
import numpy as np
import ml_dtypes
from collections import defaultdict
from contextlib import ExitStack

import concourse.bass as bass
import concourse.mybir as mybir
from concourse.bass_utils import run_bass_kernel_spmd

F32 = mybir.dt.float32
BF16 = mybir.dt.bfloat16
AF = mybir.ActivationFunctionType
ALU = mybir.AluOpType
AXX = mybir.AxisListType.X

BF = ml_dtypes.bfloat16

S = 2048
D = 2048
H = 16
HD = 128
DKV = 512
DQ = 1536
INTER = 5632
TQ = 1024
NKC = D // 128
NIC = INTER // 128
NEGB = -15.0
SCALE = 1.0 / 16.0
SHUF_SWAP = [i ^ 1 for i in range(32)]

_BCOLS = {}
_nb = 0
def _bc(name, n):
    global _nb
    _BCOLS[name] = _nb
    _nb += n
_bc("bdkv", 4); _bc("bdq", 12); _bc("buk", 16); _bc("buq", 16); _bc("bkr", 16)
_bc("bo", 16); _bc("gs", 44); _bc("us", 44); _bc("ds", 16); _bc("negb", 1)
_bc("eps", 1)
NBIAS = _nb


class Sched:
    ENGS = ("pe", "dv", "ac", "sy", "gp")

    def __init__(self):
        self.ops = []
        self.cnt = defaultdict(int)

    def do(self, eng, fn, inc=None, n=1):
        assert eng in self.ENGS
        self.ops.append((eng, "op", fn, (inc, n) if inc else None))
        if inc:
            self.cnt[inc] += n
            return self.cnt[inc]
        return None

    def wait(self, eng, sem, val=None):
        if val is None:
            val = self.cnt[sem]
        if val is None or val <= 0:
            return
        self.ops.append((eng, "wait", sem, val))

    def now(self, sem):
        return self.cnt[sem]

    def barrier(self):
        snap = {s: self.cnt[s] for s in ("PE", "DV", "AC", "LD", "WT")}
        for e in self.ENGS:
            for s_, v in snap.items():
                self.wait(e, s_, v)


class Arena:

    def __init__(self, handle, total):
        self.ap_full = handle.ap()
        self.total = total
        self.off = 0
        self.marks = []

    def bf(self, n):
        n = (n + 63) & ~63
        assert self.off + n <= self.total, f"arena overflow {self.off}+{n}"
        ap = self.ap_full[:, self.off:self.off + n]
        self.off += n
        return ap

    def f32(self, n):
        n2 = 2 * ((n + 31) & ~31)
        assert self.off + n2 <= self.total, f"arena overflow {self.off}+{n2}"
        ap = self.ap_full[:, self.off:self.off + n2].bitcast(F32)
        self.off += n2
        return ap

    def at_bf(self, off, n):
        assert off + n <= self.total
        return self.ap_full[:, off:off + n]

    def mark(self):
        self.marks.append(self.off)

    def release(self):
        self.off = self.marks.pop()


class PsumPool:

    def __init__(self, sched, tensors):
        self.s = sched
        self.P = tensors
        self.free_at = [None] * 8

    def unit(self, u):
        return self.P[u // 2][:, (u % 2) * 512:(u % 2) * 512 + 512]

    def acquire(self, units):
        for u in units:
            fa = self.free_at[u]
            if fa:
                self.s.wait("pe", fa[0], fa[1])
                self.free_at[u] = None

    def release(self, units, sem, cnt):
        for u in units:
            self.free_at[u] = (sem, cnt)


def seg512(a, b):
    out = []
    while a < b:
        e = min((a // 512 + 1) * 512, b)
        out.append((a, e))
        a = e
    return out


class _Stop(Exception):
    pass


def build_program(taps=(), stop_after=None):
    nc = bass.Bass("TRN2", target_bir_lowering=False)
    taps = set(taps)

    x_d = nc.dram_tensor("x", [S, D], F32, kind="ExternalInput").ap()
    wdkv_d = nc.dram_tensor("wdkv", [D, DKV], F32, kind="ExternalInput").ap()
    wdq_d = nc.dram_tensor("wdq", [D, DQ], F32, kind="ExternalInput").ap()
    wuk_d = nc.dram_tensor("wuk", [DKV, D], F32, kind="ExternalInput").ap()
    wuv_d = nc.dram_tensor("wuv", [DKV, D], F32, kind="ExternalInput").ap()
    wuq_d = nc.dram_tensor("wuq", [DQ, D], F32, kind="ExternalInput").ap()
    wkr_d = nc.dram_tensor("wkr", [D, D], F32, kind="ExternalInput").ap()
    wo_d = nc.dram_tensor("wo", [D, D], F32, kind="ExternalInput").ap()
    wg_d = nc.dram_tensor("wg", [D, INTER], F32, kind="ExternalInput").ap()
    wu_d = nc.dram_tensor("wu", [D, INTER], F32, kind="ExternalInput").ap()
    wd_d = nc.dram_tensor("wd", [INTER, D], F32, kind="ExternalInput").ap()
    biasT_d = nc.dram_tensor("biasT", [128, NBIAS], F32, kind="ExternalInput").ap()
    ident_d = nc.dram_tensor("ident", [128, 128], BF16, kind="ExternalInput").ap()
    tri_d = nc.dram_tensor("tri", [128, 128], BF16, kind="ExternalInput").ap()
    pmask_d = nc.dram_tensor("pmask", [128, 128], BF16, kind="ExternalInput").ap()
    c2k_d = nc.dram_tensor("c2k", [128, S], BF16, kind="ExternalInput").ap()
    s2k_d = nc.dram_tensor("s2k", [128, S], BF16, kind="ExternalInput").ap()
    buv_d = nc.dram_tensor("buvrow", [1, D], BF16, kind="ExternalInput").ap()
    ctx_hbm = nc.dram_tensor(
        "ctxscr", [H * HD, TQ], BF16,
        kind=("ExternalOutput" if "ctx" in taps else "Internal")).ap()
    out_d = nc.dram_tensor("out", [TQ, D], F32, kind="ExternalOutput").ap()
    lscr_d = nc.dram_tensor("lscr", [1, TQ], F32, kind="Internal").ap()

    tap_d = {}
    def decl_tap(name, shape, dt):
        if name in taps:
            tap_d[name] = nc.dram_tensor("tap_" + name, shape, dt,
                                         kind="ExternalOutput").ap()
    decl_tap("xnT", [128, NKC * S], BF16)
    decl_tap("ckvT", [128, 4 * S], BF16)
    decl_tap("cqT", [128, 12 * TQ], BF16)
    decl_tap("kCT", [128, S], BF16)
    decl_tap("kRT", [128, S], BF16)
    decl_tap("vG", [128, 16 * 512], BF16)
    decl_tap("qCT", [128, TQ], BF16)
    decl_tap("qRT", [128, TQ], BF16)
    decl_tap("sumT", [128, NKC * TQ], BF16)
    decl_tap("sumT2", [128, NKC * TQ], BF16)

    sch = Sched()

    with ExitStack() as es:
        e = es.enter_context
        ARENA_N = 106368
        arena_h = e(nc.sbuf_tensor("arena", [128, ARENA_N], BF16))
        A = Arena(arena_h, ARENA_N)
        P = [e(nc.psum_tensor(f"psum{i}", [128, 1024], F32)).ap()
             for i in range(4)]
        pool = PsumPool(sch, P)
        sems = {s_: e(nc.semaphore("sem" + s_))
                for s_ in ("LD", "WT", "PE", "DV", "AC")}

        biases = A.f32(NBIAS)
        ident = A.bf(128)
        tri = A.bf(128)
        pmask = A.bf(128)
        ones_bf = A.bf(64)[:, 0:1]
        ones1_bf = A.bf(128)[0:1, :]
        ones1_f = A.f32(128)[0:1, :]
        xnT_off = A.off
        xnT = A.bf(NKC * S)

        def xnt(kc):
            return xnT[:, kc * S:(kc + 1) * S]

        def bcol(name, i=0):
            c = _BCOLS[name] + i
            return biases[:, c:c + 1]

        def MM(out, lh, rh, start, stop, inc=None):
            return sch.do("pe", lambda t, o=out, l=lh, r=rh, s1=start, s2=stop:
                          t.matmul(o, l, r, start=s1, stop=s2), inc)

        def ACT(out, in_, func, bias=0.0, scale=1.0, accum=None, inc="AC"):
            return sch.do("ac", lambda a, o=out, i=in_, f=func, b=bias, s=scale,
                          ac_=accum: a.activation(out=o, in_=i, func=f, bias=b,
                                                  scale=s, accum_out=ac_), inc)

        def DV(meth, *args, inc="DV", **kw):
            def fn(v, m=meth, a=args, k=dict(kw)):
                return getattr(v, m)(*a, **k)
            return sch.do("dv", fn, inc)

        def DMA(q, out, in_, sem):
            return sch.do(q, lambda g, o=out, i=in_: g.dma_start(out=o, in_=i),
                          sem, 16)

        def tap_dump(name, src_ap):
            if name in tap_d:
                sch.barrier()
                DMA("sy", tap_d[name], src_ap, "LD")

        def wslice(dram, col0, ncols, nch, p=128):
            return dram[:, col0:col0 + ncols].rearrange(
                "(kc p) c -> p kc c", p=p)

        def r3(sbuf_ap, ncols):
            return sbuf_ap.rearrange("p (kc c) -> p kc c", c=ncols)

        _trunc = {}

        def ckstop(ph):
            if stop_after == ph and not _trunc:
                _trunc["n"] = len(sch.ops)
                _trunc["ld"] = sch.cnt["LD"]
                _trunc["wt"] = sch.cnt["WT"]

        DMA("sy", biases[:, 0:NBIAS], biasT_d, "LD")
        DMA("sy", ident, ident_d, "LD")
        DMA("sy", tri, tri_d, "LD")
        DMA("sy", pmask, pmask_d, "LD")
        setup_ld = sch.now("LD")
        DV("memset", ones_bf, 1.0)
        DV("memset", ones1_bf, 1.0)
        DV("memset", ones1_f, 1.0)

        A.mark()
        abc_base = A.off
        c2k = A.bf(S)
        s2k = A.bf(S)
        buv_row = A.bf(D)[0:1, :]
        ckvT = A.bf(4 * S)
        cqT = A.bf(12 * TQ)
        DMA("sy", c2k, c2k_d, "LD")
        DMA("sy", s2k, s2k_d, "LD")
        DMA("sy", buv_row, buv_d, "LD")
        tables_ld = sch.now("LD")

        ctxT = A.at_bf(abc_base, NKC * TQ)
        wo_r = [A.at_bf(abc_base + NKC * TQ + i * 2048, 2048)
                for i in range(2)]
        sumT = A.at_bf(abc_base + NKC * TQ + 4096, NKC * TQ)
        hdnB = abc_base + NKC * TQ + 4096 + NKC * TQ

        def hdnc(ic):
            if ic < 16:
                return ctxT[:, ic * TQ:(ic + 1) * TQ]
            if ic < 42:
                return A.at_bf(hdnB + (ic - 16) * TQ, TQ)
            return A.at_bf(abc_base + NKC * TQ + (ic - 42) * TQ, TQ)

        silu_scr = A.at_bf(abc_base + NKC * TQ + 2048, 2048).bitcast(F32)
        wg_r = [A.at_bf(ARENA_N - 2048, 2048), A.at_bf(ARENA_N - 4096, 2048)]
        wu_r = [A.at_bf(ARENA_N - 6144, 2048), A.at_bf(ARENA_N - 8192, 2048)]
        assert hdnB + 26 * TQ <= ARENA_N - 8192

        A.mark()
        AR = 4
        x_t = [A.f32(D) for _ in range(AR)]
        xn_tm = [A.bf(D) for _ in range(AR)]
        sfull = A.f32(32 * AR)
        st6 = [sfull[:, 32 * i:32 * i + 24] for i in range(AR)]
        st2c = [sfull[:, 32 * i + 24:32 * i + 26] for i in range(AR)]
        wdkv_all = A.bf(4 * NKC * 128)
        wdq_r = [A.bf(NKC * 128) for _ in range(4)]

        a_dv_done = {}
        a_ac_done = {}
        a_pe_done = {}
        sch.wait("dv", "LD", setup_ld)
        sch.wait("pe", "LD", setup_ld)
        sch.wait("ac", "LD", setup_ld)
        for oc in range(4):
            wt = DMA("gp", r3(wdkv_all[:, oc * NKC * 128:(oc + 1) * NKC * 128],
                              128),
                     wslice(wdkv_d, oc * 128, 128, NKC), "WT")
        wdkv_wt = sch.now("WT")
        wdq_wt = {}
        wdq_pe = {}

        def emit_tile_head(t):
            r = t % AR
            if t >= AR:
                sch.wait("sy", "DV", a_dv_done[t - AR])
            ld = DMA("sy", x_t[r], x_d[t * 128:(t + 1) * 128, :], "LD")
            sch.wait("dv", "LD", ld)
            for j in range(4):
                DV("bn_stats", st6[r][:, 6 * j:6 * j + 6],
                   x_t[r][:, 512 * j:512 * j + 512])
            dv_agg = DV("bn_aggr", st2c[r], st6[r])
            sch.wait("ac", "DV", dv_agg)
            ac_ln = ACT(st2c[r][:, 1:2], st2c[r][:, 1:2], AF.Ln,
                        bias=bcol("eps"))
            sch.wait("ac", "AC", ac_ln)
            ac_rstd = ACT(st2c[r][:, 1:2], st2c[r][:, 1:2], AF.Exp,
                          scale=-0.5)
            sch.wait("dv", "AC", ac_rstd)
            if t >= AR:
                sch.wait("dv", "PE", a_pe_done[t - AR])
            dv_app = DV("tensor_scalar", xn_tm[r], x_t[r], st2c[r][:, 0:1],
                        st2c[r][:, 1:2], ALU.subtract, ALU.mult)
            a_dv_done[t] = dv_app

        def emit_tile_tail(t):
            r = t % AR
            sch.wait("pe", "DV", a_dv_done[t])
            for half in range(2):
                units = (0, 1) if half == 0 else (2, 3)
                pool.acquire(units)
                for j in range(8):
                    kc = 8 * half + j
                    u = units[j // 4]
                    dst = pool.unit(u)[:, (j % 4) * 128:(j % 4) * 128 + 128]
                    pe_last = MM(dst, xn_tm[r][:, kc * 128:(kc + 1) * 128],
                                 ident, start=(j % 4 == 0), stop=(j % 4 == 3),
                                 inc="PE" if j == 7 else None)
                if half == 1:
                    a_pe_done[t] = pe_last
                sch.wait("ac", "PE", pe_last)
                pv = P[0] if half == 0 else P[1]
                src_ = pv.rearrange("p (kc c) -> p kc c", c=128)
                dst_ = xnT.rearrange("p (kc tt) -> p kc tt", tt=S)[
                    :, 8 * half:8 * half + 8, t * 128:(t + 1) * 128]
                ac_ev = ACT(dst_, src_, AF.Identity)
                pool.release(units, "AC", ac_ev)
            a_ac_done[t] = ac_ev

        def emit_wdq_dma(n, oc):
            prev = wdq_pe.get((n, oc - 4)) if oc >= 4 else (
                wdq_pe.get((0, oc + 8)) if n == 1 else None)
            if prev:
                sch.wait("gp", "PE", prev)
            wdq_wt[(n, oc)] = DMA("gp", r3(wdq_r[oc % 4], 128),
                                  wslice(wdq_d, oc * 128, 128, NKC), "WT")

        def emit_bsec(n):
            sch.wait("pe", "AC", a_ac_done[4 * n + 3])
            if n == 0:
                sch.wait("pe", "WT", wdkv_wt)
            chunks = [("kv", oc) for oc in range(4)]
            if n < 2:
                chunks += [("q", oc) for oc in range(12)]
                for oc in range(4):
                    emit_wdq_dma(n, oc)
            for k, (kind, oc) in enumerate(chunks):
                u = 4 + k % 4
                pool.acquire((u,))
                if kind == "q":
                    sch.wait("pe", "WT", wdq_wt[(n, oc)])
                for kc in range(NKC):
                    lh = (wdkv_all[:, (oc * NKC + kc) * 128:
                                   (oc * NKC + kc) * 128 + 128]
                          if kind == "kv" else
                          wdq_r[oc % 4][:, kc * 128:(kc + 1) * 128])
                    pe_l = MM(pool.unit(u), lh,
                              xnt(kc)[:, n * 512:(n + 1) * 512],
                              start=(kc == 0), stop=(kc == NKC - 1),
                              inc="PE" if kc == NKC - 1 else None)
                if kind == "q":
                    wdq_pe[(n, oc)] = pe_l
                    if oc + 4 < 12:
                        emit_wdq_dma(n, oc + 4)
                sch.wait("ac", "PE", pe_l)
                if kind == "kv":
                    dst = ckvT[:, oc * S + n * 512: oc * S + (n + 1) * 512]
                    b = bcol("bdkv", oc)
                else:
                    dst = cqT[:, oc * TQ + n * 512: oc * TQ + (n + 1) * 512]
                    b = bcol("bdq", oc)
                ac_ev = ACT(dst, pool.unit(u), AF.Identity, bias=b)
                pool.release((u,), "AC", ac_ev)

        for t in range(4):
            emit_tile_head(t)
            emit_tile_tail(t)
        for n in range(4):
            for t in range(4 * n + 4, min(4 * n + 8, 16)):
                emit_tile_head(t)
            emit_bsec(n)
            for t in range(4 * n + 4, min(4 * n + 8, 16)):
                emit_tile_tail(t)
        A.release()
        tap_dump("xnT", xnT)
        tap_dump("ckvT", ckvT)
        tap_dump("cqT", cqT)
        ckstop("B")

        sch.barrier()
        A.mark()
        kRT = [A.bf(S), A.bf(S)]
        kCT = [A.bf(S), A.bf(S)]
        vG = A.bf(16 * 512)
        qCT = [A.bf(TQ), A.bf(TQ)]
        qRT = [A.bf(TQ), A.bf(TQ)]
        swp = A.bf(S)
        Et = [A.bf(TQ), A.bf(TQ)]
        linv_rep = A.f32(TQ)
        linv_bf = A.bf(TQ)
        ctx_h = A.bf(TQ)
        ctx_n = A.bf(TQ)
        wkr_r = [A.bf(NKC * 128), A.bf(NKC * 128)]
        wuk_r = [A.bf(4 * 128), A.bf(4 * 128)]
        wuv_r = A.bf(4 * 512)
        wuq_r = [A.bf(12 * 128), A.bf(12 * 128)]

        sch.wait("dv", "LD", tables_ld)
        sch.wait("pe", "LD", tables_ld)

        v_pe_done = {}
        att_end_pe = {}
        kr_pe_d = {}
        kc_pe_d = {}
        q_pe_d = {}
        dv_qrope_d = {}
        ac_kc_d = {}
        dv_ctx_d = {}
        ac_cu_d = {}
        ac_ex_d = {}
        bc_pe_d = {}
        ctx_store = {}
        wo_wt = {}

        def emit_vgroup(g):
            if g > 0:
                sch.wait("gp", "PE", v_pe_done[g - 1])
            wt = DMA("gp", r3(wuv_r, 512), wslice(wuv_d, g * 512, 512, 4),
                     "WT")
            sch.wait("pe", "WT", wt)
            if g > 0:
                sch.wait("ac", "PE", att_end_pe[4 * g - 1])
            uperm = (0, 1, 2, 3, 6, 7, 4, 5)
            for kb in range(16):
                u = uperm[kb % 8]
                pool.acquire((u,))
                for kc in range(4):
                    MM(pool.unit(u),
                       ckvT[:, kc * S + kb * 128: kc * S + kb * 128 + 128],
                       wuv_r[:, kc * 512:(kc + 1) * 512],
                       start=(kc == 0), stop=False)
                pe_l = MM(pool.unit(u), ones1_bf,
                          buv_row[:, g * 512:(g + 1) * 512],
                          start=False, stop=True, inc="PE")
                sch.wait("ac", "PE", pe_l)
                ac_ev = ACT(vG[:, kb * 512:(kb + 1) * 512], pool.unit(u),
                            AF.Identity)
                pool.release((u,), "AC", ac_ev)
            v_pe_done[g] = sch.now("PE")
            if g == 0:
                tap_dump("vG", vG)

        def emit_prod(hh):
            r = hh % 2
            if hh >= 2:
                sch.wait("ac", "PE", att_end_pe[hh - 2])
                sch.wait("dv", "PE", att_end_pe[hh - 2])
                sch.wait("gp", "PE", kr_pe_d[hh - 2])
            wt = DMA("gp", r3(wkr_r[r], 128),
                     wslice(wkr_d, hh * 128, 128, NKC), "WT")
            units = (0, 1, 2, 3)
            pool.acquire(units)
            sch.wait("pe", "WT", wt)
            for kc in range(NKC):
                lh = wkr_r[r][:, kc * 128:(kc + 1) * 128]
                for n in range(4):
                    pe_l = MM(pool.unit(units[n]),
                              lh, xnt(kc)[:, n * 512:(n + 1) * 512],
                              start=(kc == 0), stop=(kc == NKC - 1),
                              inc="PE" if (kc == NKC - 1 and n == 3) else None)
            kr_pe_d[hh] = pe_l
            sch.wait("ac", "PE", pe_l)
            for half in range(2):
                ac_ev = ACT(kRT[r][:, half * 1024:half * 1024 + 1024],
                            P[units[2 * half] // 2], AF.Identity,
                            bias=bcol("bkr", hh))
            pool.release(units, "AC", ac_ev)
            sch.wait("dv", "AC", ac_ev)
            DV("stream_shuffle", swp, kRT[r], SHUF_SWAP)
            DV("tensor_tensor", kRT[r], kRT[r], c2k, ALU.mult)
            DV("tensor_tensor", swp, swp, s2k, ALU.mult)
            DV("tensor_tensor", kRT[r], kRT[r], swp, ALU.add)
            if hh >= 2:
                sch.wait("gp", "PE", kc_pe_d[hh - 2])
            wt = DMA("gp", r3(wuk_r[r], 128),
                     wslice(wuk_d, hh * 128, 128, 4), "WT")
            units = (4, 5, 6, 7)
            pool.acquire(units)
            sch.wait("pe", "WT", wt)
            for kc in range(4):
                lh = wuk_r[r][:, kc * 128:(kc + 1) * 128]
                for n in range(4):
                    pe_l = MM(pool.unit(units[n]), lh,
                              ckvT[:, kc * S + n * 512: kc * S + n * 512 + 512],
                              start=(kc == 0), stop=(kc == 3),
                              inc="PE" if (kc == 3 and n == 3) else None)
            kc_pe_d[hh] = pe_l
            sch.wait("ac", "PE", pe_l)
            for half in range(2):
                ac_kc = ACT(kCT[r][:, half * 1024:half * 1024 + 1024],
                            P[units[2 * half] // 2], AF.Identity,
                            bias=bcol("buk", hh))
            pool.release(units, "AC", ac_kc)
            ac_kc_d[hh] = ac_kc
            if hh >= 2:
                sch.wait("gp", "PE", q_pe_d[hh - 2])
            wt = DMA("gp", r3(wuq_r[r], 128),
                     wslice(wuq_d, hh * 128, 128, 12), "WT")
            units = (0, 1)
            pool.acquire(units)
            sch.wait("pe", "WT", wt)
            for kc in range(12):
                lh = wuq_r[r][:, kc * 128:(kc + 1) * 128]
                for n in range(2):
                    pe_l = MM(pool.unit(units[n]), lh,
                              cqT[:, kc * TQ + n * 512: kc * TQ + n * 512 + 512],
                              start=(kc == 0), stop=(kc == 11),
                              inc="PE" if (kc == 11 and n == 1) else None)
            q_pe_d[hh] = pe_l
            sch.wait("ac", "PE", pe_l)
            ac_q = ACT(qCT[r], P[0], AF.Identity, bias=bcol("buq", hh))
            pool.release(units, "AC", ac_q)
            sch.wait("dv", "AC", ac_q)
            DV("stream_shuffle", swp[:, 0:TQ], qCT[r], SHUF_SWAP)
            DV("tensor_tensor", qRT[r], qCT[r], c2k[:, 0:TQ], ALU.mult)
            DV("tensor_tensor", swp[:, 0:TQ], swp[:, 0:TQ], s2k[:, 0:TQ],
               ALU.mult)
            dv_qrope_d[hh] = DV("tensor_tensor", qRT[r], qRT[r],
                                swp[:, 0:TQ], ALU.add)
            if hh == 0:
                tap_dump("kRT", kRT[r])
                tap_dump("kCT", kCT[r])
                tap_dump("qCT", qCT[r])
                tap_dump("qRT", qRT[r])

        def emit_tail_b(hh):
            pool.acquire((6, 7))
            MM(P[3][:, 0:512], ones1_bf, linv_bf[0:1, 0:512],
               start=True, stop=True)
            bc_pe = MM(P[3][:, 512:1024], ones1_bf, linv_bf[0:1, 512:1024],
                       start=True, stop=True, inc="PE")
            bc_pe_d[hh] = bc_pe
            sch.wait("dv", "PE", bc_pe)
            sch.wait("dv", "AC", ac_cu_d[hh])
            if hh < 15:
                if hh >= 1:
                    sch.wait("dv", "LD", ctx_store[hh - 1])
                dv_ctx = DV("tensor_tensor", ctx_n, ctx_h, P[3], ALU.mult)
                dv_ctx_d[hh] = dv_ctx
                sch.wait("sy", "DV", dv_ctx)
                ctx_store[hh] = DMA("sy", ctx_hbm[hh * 128:(hh + 1) * 128, :],
                                    ctx_n, "LD")
            else:
                sch.wait("dv", "PE", q_pe_d[15])
                dv_ctx = DV("tensor_tensor", ctxT[:, 15 * TQ:16 * TQ],
                            ctx_h, P[3], ALU.mult)
                dv_ctx_d[hh] = dv_ctx
            pool.release((6, 7), "DV", dv_ctx)

        def emit_att(hh):
            oc = hh % 4
            r = hh % 2
            if hh >= 1:
                emit_tail_b(hh - 1)
            if hh == 15:
                sch.wait("sy", "PE", q_pe_d[15])
                sch.wait("sy", "DV", dv_qrope_d[15])
                sch.wait("gp", "PE", q_pe_d[15])
                sch.wait("gp", "DV", dv_qrope_d[15])
                for i in range(2):
                    wo_wt[i] = DMA("gp", r3(wo_r[i], 128),
                                   wslice(wo_d, i * 128, 128, NKC), "WT")
                for c in range(15):
                    q = "sy" if c % 2 == 0 else "gp"
                    sch.wait(q, "LD", ctx_store[c])
                    DMA(q, ctxT[:, c * TQ:(c + 1) * TQ],
                        ctx_hbm[c * 128:(c + 1) * 128, :],
                        "LD" if q == "sy" else "WT")
            pool.acquire((0, 1, 2, 3))
            sch.wait("pe", "DV", dv_qrope_d[hh])
            sch.wait("pe", "AC", ac_kc_d[hh])
            ctxl_acquired = [False]
            exp_done = {}
            ctxl_done = {}
            last_kb_bank = {0: 11, 1: 15}

            def emit_ctxl(kb):
                j0 = kb if kb < 8 else kb - 8
                q0 = j0 * 128
                par = kb % 2
                segs = seg512(q0, TQ)
                if not ctxl_acquired[0]:
                    pool.acquire((4, 5, 6, 7))
                    ctxl_acquired[0] = True
                sch.wait("pe", "AC", exp_done[kb])
                for a_, b_ in segs:
                    bank = a_ // 512
                    MM(P[2][:, a_:b_],
                       vG[:, kb * 512 + oc * 128: kb * 512 + oc * 128 + 128],
                       Et[par][:, a_:b_], start=(kb == 0),
                       stop=(kb == last_kb_bank[bank]))
                for i_, (a_, b_) in enumerate(segs):
                    bank = a_ // 512
                    pe_cl = MM(P[3][0:1, a_:b_], ones_bf, Et[par][:, a_:b_],
                               start=(kb == 0),
                               stop=(kb == last_kb_bank[bank]),
                               inc="PE" if i_ == len(segs) - 1 else None)
                ctxl_done[kb] = pe_cl

            for kb in range(16):
                j0 = kb if kb < 8 else kb - 8
                q0 = j0 * 128
                par = kb % 2
                Spar = P[par]
                if kb >= 2:
                    sch.wait("pe", "AC", exp_done[kb - 2])
                segs = seg512(q0, TQ)
                q0bank = q0 // 512
                mk = tri if kb < 8 else pmask
                MM(Spar[:, q0:q0 + 128], ident, mk, start=True, stop=False)
                for a_, b_ in segs:
                    MM(Spar[:, a_:b_],
                       kCT[r][:, kb * 128:kb * 128 + 128],
                       qCT[r][:, a_:b_],
                       start=(a_ // 512 != q0bank), stop=False)
                for i_, (a_, b_) in enumerate(segs):
                    pe_sc = MM(Spar[:, a_:b_],
                               kRT[r][:, kb * 128:kb * 128 + 128],
                               qRT[r][:, a_:b_], start=False, stop=True,
                               inc="PE" if i_ == len(segs) - 1 else None)
                sch.wait("ac", "PE", pe_sc)
                if kb >= 2:
                    sch.wait("ac", "PE", ctxl_done[kb - 2])
                ac_exp = ACT(Et[par][:, q0:TQ], Spar[:, q0:TQ], AF.Exp,
                             bias=bcol("negb"), scale=SCALE)
                exp_done[kb] = ac_exp
                if kb >= 1:
                    emit_ctxl(kb - 1)
            emit_ctxl(15)
            sch.wait("ac", "PE", ctxl_done[15])
            if hh >= 1:
                sch.wait("ac", "PE", bc_pe_d[hh - 1])
            ac_ln = ACT(linv_rep[0:1, :], P[3][0:1, :], AF.Ln)
            sch.wait("ac", "AC", ac_ln)
            ac_ex_d[hh] = ACT(linv_bf[0:1, :], linv_rep[0:1, :], AF.Exp,
                              scale=-1.0)
            if hh >= 1:
                sch.wait("ac", "DV", dv_ctx_d[hh - 1])
            ac_cu = ACT(ctx_h, P[2], AF.Identity)
            ac_cu_d[hh] = ac_cu
            att_end_pe[hh] = ctxl_done[15]
            pool.release((0, 1), "AC", exp_done[14])
            pool.release((2, 3), "AC", exp_done[15])
            pool.release((4, 5), "AC", ac_cu)
            pool.release((6, 7), "AC", ac_ex_d[hh])

        for g in range(4):
            emit_vgroup(g)
            for oc in range(4):
                hh = 4 * g + oc
                emit_prod(hh)
                if oc >= 1:
                    emit_att(hh - 1)
            emit_att(4 * g + 3)
        emit_tail_b(15)
        A.release()
        A.release()
        ckstop("C")

        rl_ld = sch.now("LD")
        rl_wt = sch.now("WT")
        wgwu_wt = {}
        for ic in range(2):
            wgwu_wt[("g", ic)] = DMA("gp", r3(wg_r[ic], 128),
                                     wslice(wg_d, ic * 128, 128, NKC), "WT")
            wgwu_wt[("u", ic)] = DMA("gp", r3(wu_r[ic], 128),
                                     wslice(wu_d, ic * 128, 128, NKC), "WT")
        sch.wait("pe", "LD", rl_ld)
        sch.wait("pe", "WT", rl_wt)
        sch.wait("pe", "DV", dv_ctx_d[15])
        d_pe_done = {}
        for oc in range(NKC):
            r = oc % 2
            if oc >= 2:
                sch.wait("gp", "PE", d_pe_done[oc - 2])
                wo_wt[oc] = DMA("gp", r3(wo_r[r], 128),
                                wslice(wo_d, oc * 128, 128, NKC), "WT")
            units = (0, 1) if r == 0 else (2, 3)
            pool.acquire(units)
            sch.wait("pe", "WT", wo_wt[oc])
            for kc in range(NKC):
                lh = wo_r[r][:, kc * 128:(kc + 1) * 128]
                for n in range(2):
                    pe_l = MM(pool.unit(units[n]), lh,
                              ctxT[:, kc * TQ + n * 512: kc * TQ + n * 512 + 512],
                              start=(kc == 0), stop=(kc == NKC - 1),
                              inc="PE" if (kc == NKC - 1 and n == 1) else None)
            d_pe_done[oc] = pe_l
            sch.wait("ac", "PE", pe_l)
            ac_ev = ACT(sumT[:, oc * TQ:(oc + 1) * TQ], P[units[0] // 2],
                        AF.Identity, bias=bcol("bo", oc))
            pool.release(units, "AC", ac_ev)
        tap_dump("sumT", sumT)
        ckstop("D")

        e_pe_done = {}
        e_dv_done = {}
        for ic in range(NIC):
            er = ic % 2
            if ic >= 2:
                sch.wait("gp", "PE", e_pe_done[("g", ic - 2)])
                wt = DMA("gp", r3(wg_r[er], 128),
                         wslice(wg_d, ic * 128, 128, NKC), "WT")
            else:
                wt = wgwu_wt[("g", ic)]
            units = (0, 1) if ic % 2 == 0 else (4, 5)
            uu = (2, 3) if ic % 2 == 0 else (6, 7)
            pool.acquire(units)
            sch.wait("pe", "WT", wt)
            for kc in range(NKC):
                lh = wg_r[er][:, kc * 128:(kc + 1) * 128]
                for n in range(2):
                    pe_g = MM(pool.unit(units[n]), lh,
                              xnt(kc)[:, n * 512:(n + 1) * 512],
                              start=(kc == 0), stop=(kc == NKC - 1),
                              inc="PE" if (kc == NKC - 1 and n == 1) else None)
            e_pe_done[("g", ic)] = pe_g
            if ic >= 2:
                sch.wait("gp", "PE", e_pe_done[("u", ic - 2)])
                wt = DMA("gp", r3(wu_r[er], 128),
                         wslice(wu_d, ic * 128, 128, NKC), "WT")
            else:
                wt = wgwu_wt[("u", ic)]
            pool.acquire(uu)
            sch.wait("pe", "WT", wt)
            for kc in range(NKC):
                lh = wu_r[er][:, kc * 128:(kc + 1) * 128]
                for n in range(2):
                    pe_u = MM(pool.unit(uu[n]), lh,
                              xnt(kc)[:, n * 512:(n + 1) * 512],
                              start=(kc == 0), stop=(kc == NKC - 1),
                              inc="PE" if (kc == NKC - 1 and n == 1) else None)
            e_pe_done[("u", ic)] = pe_u
            sch.wait("ac", "PE", pe_g)
            if ic >= 1:
                sch.wait("ac", "DV", e_dv_done[ic - 1])
            ac_si = ACT(silu_scr, P[units[0] // 2], AF.Silu,
                        scale=bcol("gs", ic))
            sch.wait("dv", "PE", pe_u)
            sch.wait("dv", "AC", ac_si)
            dv_h = DV("scalar_tensor_tensor", hdnc(ic), P[uu[0] // 2],
                      bcol("us", ic), silu_scr, ALU.mult, ALU.mult)
            e_dv_done[ic] = dv_h
            pool.release(units, "AC", ac_si)
            pool.release(uu, "DV", dv_h)
        wd_r = [A.at_bf(xnT_off, NIC * 128),
                A.at_bf(xnT_off + NIC * 128 + 64, NIC * 128)]
        xnT_dead = e_pe_done[("u", NIC - 1)]
        sch.wait("gp", "PE", xnT_dead)
        wd0s = r3(wd_r[0], 128)
        wd0d = wslice(wd_d, 0, 128, NIC)
        wd0_wt = {}
        for s_ in range(4):
            a_, b_ = 11 * s_, 11 * (s_ + 1)
            wd0_wt[s_] = DMA("gp", wd0s[:, a_:b_, :], wd0d[:, a_:b_, :], "WT")
        dn_pe_done = {}
        for oc in range(NKC):
            r = oc % 2
            if oc == 0:
                wt = None
            else:
                if oc >= 2:
                    sch.wait("gp", "PE", dn_pe_done[oc - 2])
                wt = DMA("gp", r3(wd_r[r], 128),
                         wslice(wd_d, oc * 128, 128, NIC), "WT")
            units = (0, 1) if r == 0 else (2, 3)
            pool.acquire(units)
            if wt is not None:
                sch.wait("pe", "WT", wt)
            for ic in range(NIC):
                if oc == 0:
                    if ic % 11 == 0:
                        sch.wait("pe", "WT", wd0_wt[ic // 11])
                    if ic % 8 == 0 or ic == NIC - 1:
                        sch.wait("pe", "DV",
                                 e_dv_done[min(ic + 7, NIC - 1)])
                lh = wd_r[r][:, ic * 128:(ic + 1) * 128]
                for n in range(2):
                    pe_l = MM(pool.unit(units[n]), lh,
                              hdnc(ic)[:, n * 512:(n + 1) * 512],
                              start=(ic == 0), stop=(ic == NIC - 1),
                              inc="PE" if (ic == NIC - 1 and n == 1) else None)
            dn_pe_done[oc] = pe_l
            sch.wait("dv", "PE", pe_l)
            dv_ev = DV("scalar_tensor_tensor", sumT[:, oc * TQ:(oc + 1) * TQ],
                       P[units[0] // 2], bcol("ds", oc),
                       sumT[:, oc * TQ:(oc + 1) * TQ], ALU.mult, ALU.add)
            pool.release(units, "DV", dv_ev)
        tap_dump("sumT2", sumT)
        ckstop("E")

        sch.barrier()
        FR = 4
        x_rt = [A.at_bf(xnT_off + i * 4096, 4096).bitcast(F32)
                for i in range(FR)]
        stage = [A.at_bf(xnT_off + 16384 + i * 4096, 4096).bitcast(F32)
                 for i in range(FR)]
        f_dv_done = {}
        f_st = {}
        for tb in range(8):
            r = tb % FR
            if tb >= FR:
                sch.wait("gp", "DV", f_dv_done[tb - FR])
            ldx = DMA("gp", x_rt[r], x_d[tb * 128:(tb + 1) * 128, :], "WT")
            units = (0, 1, 2, 3) if tb % 2 == 0 else (4, 5, 6, 7)
            pool.acquire(units)
            for oc in range(NKC):
                u = units[oc // 4]
                dst = pool.unit(u)[:, (oc % 4) * 128:(oc % 4) * 128 + 128]
                pe_l = MM(dst,
                          sumT[:, oc * TQ + tb * 128: oc * TQ + tb * 128 + 128],
                          ident, start=(oc % 4 == 0), stop=(oc % 4 == 3),
                          inc="PE" if oc == NKC - 1 else None)
            sch.wait("dv", "PE", pe_l)
            sch.wait("dv", "WT", ldx)
            if tb >= FR:
                sch.wait("dv", "LD", f_st[tb - FR])
            for half in range(2):
                dv_f = DV("tensor_tensor",
                          stage[r].rearrange("p (h c) -> p h c", h=2)[:, half, :],
                          P[units[2 * half] // 2],
                          x_rt[r].rearrange("p (h c) -> p h c", h=2)[:, half, :],
                          ALU.add)
            f_dv_done[tb] = dv_f
            pool.release(units, "DV", dv_f)
            sch.wait("sy", "DV", dv_f)
            f_st[tb] = DMA("sy", out_d[tb * 128:(tb + 1) * 128, :], stage[r],
                           "LD")
        sch.wait("sy", "LD")
        sch.wait("sy", "WT")

        if _trunc:
            sch.ops = sch.ops[:_trunc["n"]]
            if _trunc["ld"]:
                sch.ops.append(("sy", "wait", "LD", _trunc["ld"]))
            if _trunc["wt"]:
                sch.ops.append(("sy", "wait", "WT", _trunc["wt"]))
        block = e(nc.Block())

        def run_for(key):
            pad_nop = key == "dv"

            def prog(eng):
                prev_was_op = False
                for op in sch.ops:
                    if op[0] != key:
                        continue
                    if op[1] == "wait":
                        eng.wait_ge(sems[op[2]], op[3])
                        prev_was_op = False
                    else:
                        if pad_nop and prev_was_op:
                            eng.engine_nop()
                        inst = op[2](eng)
                        if op[3] is not None:
                            inst.then_inc(sems[op[3][0]], op[3][1])
                        prev_was_op = True
            return prog
        block.tensor(run_for("pe"))
        block.vector(run_for("dv"))
        block.scalar(run_for("ac"))
        block.sync(run_for("sy"))
        block.gpsimd(run_for("gp"))

    return nc



def _host_prep(inputs):
    x_full = np.asarray(inputs["L0_attn_h"], np.float32)
    shared = {
        "wdkv": np.ascontiguousarray(np.asarray(inputs["L0_attn_W_DKV"], np.float32)),
        "wdq": np.ascontiguousarray(np.asarray(inputs["L0_attn_W_DQ"], np.float32)),
        "wuk": np.ascontiguousarray(np.asarray(inputs["L0_attn_W_UK"], np.float32)),
        "wuv": np.ascontiguousarray(np.asarray(inputs["L0_attn_W_UV"], np.float32)),
        "wuq": np.ascontiguousarray(np.asarray(inputs["L0_attn_W_UQ"], np.float32)),
        "wkr": np.ascontiguousarray(np.asarray(inputs["L0_attn_W_KR"], np.float32)),
        "wo": np.ascontiguousarray(np.asarray(inputs["L0_attn_W_O"], np.float32)),
        "wg": np.ascontiguousarray(np.asarray(inputs["L0_ffn_gate_w"], np.float32).T),
        "wu": np.ascontiguousarray(np.asarray(inputs["L0_ffn_up_w"], np.float32).T),
        "wd": np.ascontiguousarray(np.asarray(inputs["L0_ffn_down_w"], np.float32).T),
        "ident": np.eye(128, dtype=np.float32).astype(BF),
        "buvrow": np.asarray(inputs["L0_attn_b_UV"], np.float32).astype(BF).reshape(1, D),
    }
    biasT = np.zeros((128, NBIAS), np.float32)
    def put(name, vec):
        vec = np.asarray(vec, np.float32)
        n = vec.size // 128
        biasT[:, _BCOLS[name]:_BCOLS[name] + n] = vec.reshape(n, 128).T
    put("bdkv", inputs["L0_attn_b_DKV"])
    put("bdq", inputs["L0_attn_b_DQ"])
    put("buk", inputs["L0_attn_b_UK"])
    put("buq", inputs["L0_attn_b_UQ"])
    put("bkr", inputs["L0_attn_b_KR"])
    put("bo", inputs["L0_attn_b_O"])
    put("gs", inputs["L0_ffn_gate_s"])
    put("us", inputs["L0_ffn_up_s"])
    put("ds", inputs["L0_ffn_down_s"])
    biasT[:, _BCOLS["negb"]] = NEGB
    biasT[:, _BCOLS["eps"]] = 1e-5
    shared["biasT"] = biasT
    rr = np.arange(128)
    shared["tri"] = np.where(rr[:, None] <= rr[None, :], 0.0,
                             -29952.0).astype(np.float32).astype(BF)

    in_maps = []
    for c in range(8):
        b, p = c // 2, c % 2
        ob = [2 * j + p for j in range(8)] + [2 * j + (1 - p) for j in range(8)]
        perm = np.concatenate([np.arange(o * 128, (o + 1) * 128) for o in ob])
        m = dict(shared)
        m["x"] = np.ascontiguousarray(x_full[b][perm])
        m["pmask"] = np.full((128, 128), 0.0 if p == 1 else -29952.0,
                             np.float32).astype(BF)
        pos = perm.astype(np.float64)
        inv = 1.0 / (10000.0 ** (np.arange(64, dtype=np.float64) * 2.0 / 128.0))
        ang = inv[:, None] * pos[None, :]
        c2 = np.repeat(np.cos(ang), 2, axis=0)
        s2 = np.repeat(np.sin(ang), 2, axis=0)
        s2[0::2] *= -1.0
        m["c2k"] = c2.astype(np.float32).astype(BF)
        m["s2k"] = s2.astype(np.float32).astype(BF)
        in_maps.append(m)
    return in_maps


def kernel(**inputs):
    in_maps = _host_prep(inputs)
    nc = build_program()
    res = run_bass_kernel_spmd(nc, in_maps, core_ids=list(range(8)))
    out = np.empty((4, S, D), np.float32)
    for c in range(8):
        b, p = c // 2, c % 2
        r = np.asarray(res.results[c]["out"], np.float32)
        for j in range(8):
            o = 2 * j + p
            out[b, o * 128:(o + 1) * 128, :] = r[j * 128:(j + 1) * 128, :]
    return out
